# version 23
# baseline (speedup 1.0000x reference)
import contextlib
import numpy as np

import concourse.bass as bass
import concourse.mybir as mybir
from concourse.bass_utils import run_bass_kernel_spmd
from concourse.alu_op_type import AluOpType as OP
from concourse import library_config

F32 = mybir.dt.float32
F16 = mybir.dt.float16
AF = mybir.ActivationFunctionType

DM, DIN, DST, NH, HD, Q = 1024, 2048, 128, 32, 64, 128
L, TOWN = 4096, 2048
NCK, OCK = L // Q, TOWN // Q
DIPP = 4480
NKK = DM // 128
NCC = DIPP // 128
RANK, DFF = 256, 4096
EPS = 1e-5


class P:
    def __init__(self, nc, block, stack):
        self.nc, self.block, self.stack = nc, block, stack
        self.ct, self.sems = {}, {}
        for name in ("qS", "qG", "sPE", "sDV", "sAC", "sGP"):
            self.sems[name] = stack.enter_context(nc.semaphore(name))
            self.ct[name] = 0

    def inc(self, instr, sem, n=1):
        self.ct[sem] += n
        instr.then_inc(self.sems[sem], n)
        return instr

    def wait(self, eng, *sems):
        for s in sems:
            if self.ct[s] > 0:
                eng.wait_ge(self.sems[s], self.ct[s])

    def mark(self, sem):
        return self.ct[sem]

    def wait_at(self, eng, sem, v):
        if v > 0:
            eng.wait_ge(self.sems[sem], v)

    def _ninstr(self):
        return sum(len(bb.instructions) for f in self.nc.m.functions
                   for bb in f.blocks)

    def add_sem(self, name):
        if name not in self.sems:
            self.sems[name] = self.stack.enter_context(self.nc.semaphore(name))
            self.ct[name] = 0
        return name

    def wait_all_dma(self, eng):
        for nm in list(self.sems):
            if nm.startswith("q"):
                self.wait(eng, nm)

    def wait_everything(self, eng):
        for nm in list(self.sems):
            self.wait(eng, nm)

    def dma(self, eng, sem, **kw):
        n0 = self._ninstr()
        ins = eng.dma_start(**kw)
        k = self._ninstr() - n0
        assert k >= 1
        ins.then_inc(self.sems[sem], 16)
        self.ct[sem] += 16 * k
        return ins

    def sb(self, name, shape, dtype=F32, stack=None):
        return (stack or self.stack).enter_context(
            self.nc.sbuf_tensor(name, shape, dtype))

    def ps(self, name, shape, dtype=F32, stack=None):
        return (stack or self.stack).enter_context(
            self.nc.psum_tensor(name, shape, dtype))


def _drive(gens):
    gens = list(gens)
    while gens:
        done = []
        for g in gens:
            try:
                next(g)
            except StopIteration:
                done.append(g)
        for g in done:
            gens.remove(g)


def hb(ap2, p=HD):
    return ap2.unsqueeze(2).broadcast_to([ap2.shape[0], ap2.shape[1], p])


def build_program(dump=(), stages=("s0", "s2z", "s2", "s4", "xc", "s5", "s5b", "s6")):
    nc = bass.Bass("TRN2", target_bir_lowering=False)
    io = {}

    def din(name, shape, dtype=F32):
        io[name] = nc.dram_tensor(name, shape, dtype, kind="ExternalInput").ap()

    def dscr(name, shape, dtype=F32):
        kind = "ExternalOutput" if name in dump else "Internal"
        io[name] = nc.dram_tensor(name, shape, dtype, kind=kind).ap()

    din("xTf", [DM, L])
    din("x_tm", [TOWN, DM])
    din("WinT3", [NCC, 128, NKK * 128])
    din("WoutT2", [2, 128, 8192])
    din("U1b", [NKK, 2, 128, 128])
    din("V1b", [2, 32, 128, 128])
    din("U2b", [32, 2, 128, 128])
    din("V2b", [2, 8, 128, 128])
    din("b1", [DFF])
    din("b2", [DM])
    din("fsel", [128, 1])
    for s in ("A", "B"):
        din("cw" + s, [2304, 4])
        din("cb" + s, [2304])
        din("dtb" + s, [NH])
        din("Alog" + s, [NH])
        din("D" + s, [NH])
        din("nw" + s, [DIN])
    io["out"] = nc.dram_tensor("out", [TOWN, DM], F32, kind="ExternalOutput").ap()

    dscr("ssz", [OCK, 128, DIN])
    dscr("sxsA", [OCK, 128, DIN])
    dscr("sxsB", [NCK, 128, DIN])
    dscr("sBA", [OCK, DST, Q])
    dscr("sBB", [NCK, DST, Q])
    dscr("sCA", [OCK, DST, Q])
    dscr("sCB", [OCK, DST, Q])
    for nm in ("sWzH", "sWzL"):
        io[nm] = nc.dram_tensor(nm, [4, 128, NKK * 512], F16,
                                kind="Internal").ap()
    for nm in ("sWfH", "sWfL"):
        io[nm] = nc.dram_tensor(nm, [19, 128, NKK * 128], F16,
                                kind="Internal").ap()
    dscr("sdtA", [NH, L])
    dscr("sdtB", [NH, L])
    dscr("srowD", [1, 2 * DIN])
    for tg in ("s4", "s5"):
        dscr("srowS" + tg, [1, 16 * NH])
        dscr("srowC" + tg, [1, 16 * NH * 128])
    io["cgin"] = nc.dram_tensor("cgin", [DST, DIN], F32, kind="Internal").ap()
    io["cgout"] = nc.dram_tensor("cgout", [2, DST, DIN], F32,
                                 kind="Internal").ap()
    dscr("sYA", [OCK, 128, DIN])
    dscr("sYS", [OCK, 128, DIN])
    dscr("sXLN", [TOWN, DM])

    with contextlib.ExitStack() as gs:
        block = gs.enter_context(nc.Block())
        p = P(nc, block, gs)
        C = {}
        C["ident"] = p.sb("ident", [128, 128])
        C["svec"] = p.sb("svec", [1, 128])
        C["b1r"] = p.sb("b1r", [128, 32])
        for nm in ("dtbA", "dtbB", "AA", "AB"):
            C[nm] = p.sb(nm + "_t", [32, 1])
        C["epsb"] = p.sb("epsb", [128, 1])
        C["cwA"] = p.sb("cwA_t", [128, 18 * 4])
        C["cwB"] = p.sb("cwB_t", [128, 18 * 4])
        C["cbA"] = p.sb("cbA_t", [128, 18])
        C["cbB"] = p.sb("cbB_t", [128, 18])

        with contextlib.ExitStack() as ms:
            C["maskA"] = p.sb("maskA", [128, 128], stack=ms)
            C["maskB"] = p.sb("maskB", [128, 128], stack=ms)
            C["nwAt"] = p.sb("nwAt", [128, DIN], stack=ms)
            C["nwBt"] = p.sb("nwBt", [128, DIN], stack=ms)
            C["dgDA"] = p.sb("dgDA", [128, DIN], stack=ms)
            C["dgDB"] = p.sb("dgDB", [128, DIN], stack=ms)
            srunA = p.sb("srunA", [DST, DIN], stack=ms)
            srunB = p.sb("srunB", [DST, DIN], stack=ms)

            emit_s0(p, io, C, ms)
            if "s2" in stages:
                with contextlib.ExitStack() as st:
                    emit_s1(p, io, C, st)
                with contextlib.ExitStack() as st:
                    emit_s2(p, io, C, st)
            gens, stacks = [], []
            if "s4" in stages:
                st4 = contextlib.ExitStack()
                gens.append(gen_ssd(p, io, C, st4, dirB=False,
                                    chunks=list(range(OCK)),
                                    compute_y=True, srun=srunA, cont=False,
                                    tag="s4", combine=False))
                stacks.append(st4)
            _drive(gens)
            for st_ in reversed(stacks):
                st_.close()
            if "xc" in stages:
                with contextlib.ExitStack() as stx:
                    emit_exchange(p, io, C, stx, srunA, srunB)
            if "s5" in stages:
                st5 = contextlib.ExitStack()
                st5b = contextlib.ExitStack()
                slog = []
                xlog = []
                T5 = {}
                g5 = gen_ssd(p, io, C, st5, dirB=True,
                             chunks=list(range(OCK - 1, -1, -1)),
                             compute_y=True, srun=srunB, cont=True,
                             tag="s5", combine=True, spill_log=slog,
                             tiles_out=T5)
                act = [g5]
                if "s5b" in stages:
                    act.append(gen_s5b(p, io, C, st5b, slog, xlog))
                g6 = gen_s6(p, io, C, T5, xlog) if "s6" in stages else None
                while act:
                    for g in list(act):
                        try:
                            next(g)
                        except StopIteration:
                            act.remove(g)
                            if g is g5 and g6 is not None:
                                act.append(g6)
                                g6 = None
                st5b.close()
                st5.close()

        @block.sync
        def _(sync):
            p.wait(sync, "qG", "qS")
    return nc, io


def emit_s0(p, io, C, ms):
    b = p.block
    s0s = contextlib.ExitStack()
    DtA = p.sb("DtA", [128, DIN], stack=s0s)
    DtB = p.sb("DtB", [128, DIN], stack=s0s)
    C["Drow"] = p.sb("Drow", [1, 2 * DIN], stack=s0s)
    ps0 = p.ps("s0_ps", [128, 512], stack=s0s)

    @b.sync
    def _(sync):
        for s in ("A", "B"):
            p.dma(sync, "qS", 
                out=C["cw" + s][:, :].rearrange("p (cc t) -> p cc t", cc=18),
                in_=io["cw" + s].rearrange("(cc p) t -> p cc t", p=128))
            p.dma(sync, "qS", 
                out=C["cb" + s][:, :],
                in_=io["cb" + s].rearrange("(cc p) -> p cc", p=128),
                allow_slow_non_contiguous=True)
            p.dma(sync, "qS", out=C["dtb" + s][:, 0:1],
                                 in_=io["dtb" + s][:].unsqueeze(1))
        p.dma(sync, "qS", out=C["svec"][0:1, 0:32],
                             in_=io["DA"][:].unsqueeze(0))
        p.dma(sync, "qS", out=C["svec"][0:1, 32:64],
                             in_=io["DB"][:].unsqueeze(0))
        p.dma(sync, "qS", out=C["svec"][0:1, 64:96],
                             in_=io["AlogA"][:].unsqueeze(0))
        p.dma(sync, "qS", out=C["svec"][0:1, 96:128],
                             in_=io["AlogB"][:].unsqueeze(0))

        p.dma(sync, "qS", out=C["b1r"][:, :],
                             in_=io["b1"].rearrange("(cc p) -> p cc", p=128),
                             allow_slow_non_contiguous=True)

    @b.vector
    def _(v):
        p.inc(v.memset(C["epsb"][:, :], EPS), "sDV")

    @b.scalar
    def _(sc):
        p.wait(sc, "qS")
        p.inc(sc.activation(C["svec"][0:1, 64:128], C["svec"][0:1, 64:128],
                            AF.Exp), "sAC")

    @b.vector
    def _(v):
        p.wait(v, "sAC")
        p.inc(v.tensor_scalar(C["svec"][0:1, 64:128], C["svec"][0:1, 64:128],
                              -1.0, None, OP.mult), "sDV")

    @b.gpsimd
    def _(g):
        p.wait(g, "qS")
        p.inc(g.memset(C["ident"][:, :], 0.0), "sGP")
        g.wait_ge(p.sems["sGP"], p.ct["sGP"])
        p.inc(g.affine_select(out=C["ident"][:, :], in_=C["ident"][:, :],
                              compare_op=OP.not_equal, fill=1.0, base=0,
                              pattern=[[-1, 128]], channel_multiplier=1), "sGP")
        g.wait_ge(p.sems["sGP"], p.ct["sGP"])
        p.inc(g.memset(C["maskA"][:, :], 0.0), "sGP")
        g.wait_ge(p.sems["sGP"], p.ct["sGP"])
        p.inc(g.affine_select(out=C["maskA"][:, :], in_=C["maskA"][:, :],
                              compare_op=OP.is_gt, fill=1.0, base=0,
                              pattern=[[-1, 128]], channel_multiplier=1), "sGP")
        g.wait_ge(p.sems["sGP"], p.ct["sGP"])
        p.inc(g.memset(C["maskB"][:, :], 1.0), "sGP")
        g.wait_ge(p.sems["sGP"], p.ct["sGP"])
        p.inc(g.affine_select(out=C["maskB"][:, :], in_=C["maskB"][:, :],
                              compare_op=OP.is_ge, fill=0.0, base=0,
                              pattern=[[-1, 128]], channel_multiplier=1), "sGP")

    @b.tensor
    def _(t):
        p.wait(t, "sDV", "sGP")
        t.transpose(ps0[0:32, 0:1], C["svec"][0:1, 64:96], C["ident"][0:1, 0:1])
        p.inc(t.transpose(ps0[0:32, 1:2], C["svec"][0:1, 96:128],
                          C["ident"][0:1, 0:1]), "sPE")

    @b.vector
    def _(v):
        p.wait(v, "sPE")
        v.tensor_copy(C["AA"][:, 0:1], ps0[0:32, 0:1])
        v.tensor_copy(C["AB"][:, 0:1], ps0[0:32, 1:2])
        v.tensor_copy(C["Drow"][0:1, 0:DIN].rearrange("o (h e) -> o h e", h=NH),
                      hb(C["svec"][0:1, 0:32]))
        p.inc(v.tensor_copy(
            C["Drow"][0:1, DIN:].rearrange("o (h e) -> o h e", h=NH),
            hb(C["svec"][0:1, 32:64])), "sDV")

    @b.sync
    def _(sync):
        p.wait(sync, "sDV")
        p.dma(sync, "qS", out=io["srowD"][:, :], in_=C["Drow"][:, :])
        p.wait(sync, "qS")
        p.dma(sync, "qS", 
            out=DtA[:, :],
            in_=io["srowD"][0:1, 0:DIN].partition_broadcast(128).squeeze(1))
        p.dma(sync, "qS", 
            out=DtB[:, :],
            in_=io["srowD"][0:1, DIN:].partition_broadcast(128).squeeze(1))
        p.dma(sync, "qS", 
            out=C["nwAt"][:, :],
            in_=io["nwA"][:].unsqueeze(0).partition_broadcast(128).squeeze(1))
        p.dma(sync, "qS", 
            out=C["nwBt"][:, :],
            in_=io["nwB"][:].unsqueeze(0).partition_broadcast(128).squeeze(1))

    @b.vector
    def _(v):
        p.wait(v, "sGP", "qS")
        ins = None
        for i in range(16):
            sl = slice(i * 128, (i + 1) * 128)
            v.tensor_tensor(C["dgDA"][:, sl], DtA[:, sl], C["ident"][:, :], OP.mult)
            ins = v.tensor_tensor(C["dgDB"][:, sl], DtB[:, sl], C["ident"][:, :],
                                  OP.mult)
        p.inc(ins, "sDV")
    s0s.close()


def emit_s1(p, io, C, st):
    b = p.block
    w32 = [p.sb("s1_w32a", [128, NKK * 128], stack=st),
           p.sb("s1_w32b", [128, NKK * 128], stack=st)]
    qs1 = [p.add_sem("qs1a"), p.add_sem("qs1b")]
    for i in range(4):
        p.add_sem("qs1z%d" % i)
    p.add_sem("qs1f")
    wh = p.sb("s1_wh", [128, NKK * 128], F16, stack=st)
    wl = p.sb("s1_wl", [128, NKK * 128], F16, stack=st)
    lm = [0, 0]
    spm = None
    for cc in range(NCC):
        par = cc % 2

        @b.sync
        def _(sync, cc=cc, par=par):
            p.wait_at(sync, "sDV", lm[par])
            if cc < 2:
                p.wait(sync, "qS", "sGP", "sDV")
            p.dma(sync, qs1[par], out=w32[par][:, :], in_=io["WinT3"][cc])
        qm = p.mark(qs1[par])

        @b.vector
        def _(v, cc=cc, par=par, qm=qm, spm=spm):
            p.wait_at(v, qs1[par], qm)
            if spm:
                p.wait(v, spm)
            p.inc(v.tensor_copy(wh[:, :], w32[par][:, :]), "sDV")
            p.wait(v, "sDV")
            p.inc(v.tensor_tensor(wl[:, :], w32[par][:, :], wh[:, :],
                                  OP.subtract), "sDV")
        lm[par] = p.mark("sDV")

        snm = ("qs1z%d" % (cc // 4)) if cc < 16 else "qs1f"

        @b.gpsimd
        def _(g, cc=cc, m=lm[par], snm=snm):
            p.wait_at(g, "sDV", m)
            if cc < 16:
                ct, l = cc // 4, cc % 4
                for wt, dst in ((wh, "sWzH"), (wl, "sWzL")):
                    p.dma(g, snm,
                          out=io[dst][ct].rearrange(
                              "p (kk l c) -> p kk l c", kk=NKK, l=4)[:, :, l, :],
                          in_=wt[:, :].rearrange("p (kk c) -> p kk c", kk=NKK))
            else:
                p.dma(g, snm, out=io["sWfH"][cc - 16], in_=wh[:, :])
                p.dma(g, snm, out=io["sWfL"][cc - 16], in_=wl[:, :])
        spm = snm


def emit_s2(p, io, C, st):
    b = p.block
    ring = [p.sb("f_ring0", [128, 18 * 520], stack=st),
            p.sb("f_ring1", [128, 18 * 520], stack=st)]
    xblk = p.sb("f_xblk", [128, NKK * 512], stack=st)
    xbh = p.sb("f_xbh", [128, NKK * 512], F16, stack=st)
    xbl = p.sb("f_xbl", [128, NKK * 512], F16, stack=st)
    wsh = [p.sb("f_wsh0", [128, NKK * 128], F16, stack=st),
           p.sb("f_wsh1", [128, NKK * 128], F16, stack=st)]
    wsl = [p.sb("f_wsl0", [128, NKK * 128], F16, stack=st),
           p.sb("f_wsl1", [128, NKK * 128], F16, stack=st)]
    wzh = p.sb("f_wzh", [128, NKK * 512], F16, stack=st)
    wzl = p.sb("f_wzl", [128, NKK * 512], F16, stack=st)
    zev2 = p.sb("f_zev2", [128, 2048], stack=st)
    cvo = [p.sb("f_cvo0", [128, 512], stack=st),
           p.sb("f_cvo1", [128, 512], stack=st)]
    cvtA = p.sb("f_cvtA", [128, 512], stack=st)
    cvtB = p.sb("f_cvtB", [128, 512], stack=st)
    dtacc = p.sb("f_dtacc", [32, TOWN], stack=st)
    pf = [p.ps("f_pf0", [128, 512], stack=st), p.ps("f_pf1", [128, 512], stack=st)]
    pz = p.ps("f_pz", [128, 2048], stack=st)

    def eap(buf, ccv):
        return buf[:, ccv * 520:ccv * 520 + 518]

    for nm in ("qX0", "qX1", "qW0", "qW1", "qZW", "qZE0", "qZE1",
               "qCV0", "qCV1", "qDT0", "qDT1", "qDP0", "qDP1", "qDS0", "qDS1"):
        p.add_sem(nm)
    dtaccm = [0]
    xmark = [0, 0]
    xsplit = [0]
    xuser = [0, 0]
    zwm = [0, 0]
    zwu = [0, 0]
    zpsa = [0, 0]
    zspill = [0, 0]
    wmark = [0, 0]
    fwu = [0, 0]
    pfev = [0, 0]
    sAm = [0] * 18
    sBm = [0] * 18
    gpm_ccv = [0] * 18
    cvt_free = [0, 0]
    cvsp = [0, 0]
    dtsp = [0, 0]
    cvj = 0

    zq = 0
    for tb in range(5):
        cur, prev = ring[tb % 2], ring[(tb + 1) % 2]
        own = tb < 4
        mini = tb == 4
        nt = 3 if mini else 512
        if True:
            par_x = tb % 2

            @b.sync
            def _(sync, tb=tb, nt=nt):
                p.wait_at(sync, "sDV", xsplit[0])
                if tb < 2:
                    p.wait(sync, "sDV")
                p.dma(sync, "qX0",
                    out=xblk[:, :].rearrange("p (kk t) -> p kk t",
                                             kk=NKK)[:, :, 0:nt],
                    in_=io["xTf"][:, tb * 512:tb * 512 + nt].rearrange(
                        "(kk p) t -> p kk t", p=128))
                xmark[0] = p.mark("qX0")

            @b.vector
            def _(v, tb=tb):
                p.wait_at(v, "qX0", xmark[0])
                p.wait_at(v, "sPE", xuser[0])
                p.inc(v.tensor_copy(xbh[:, :], xblk[:, :]), "sDV")
                p.wait(v, "sDV")
                p.inc(v.tensor_tensor(xbl[:, :], xblk[:, :], xbh[:, :],
                                      OP.subtract), "sDV")
            xsplit[0] = p.mark("sDV")

            if tb == 0:
                @b.vector
                def _(v):
                    p.wait_everything(v)
                    v.memset(ring[0][:, :], 0.0)
                    p.inc(v.memset(ring[1][:, :], 0.0), "sDV")

            if own:
                for ct in range(4):
                    wpar = ct % 2

                    @b.sync
                    def _(sync, ct=ct, wpar=wpar):
                        p.wait_at(sync, "sPE", max(zwu))
                        p.wait(sync, "qs1z%d" % ct)
                        p.dma(sync, "qZW", out=wzh[:, :], in_=io["sWzH"][ct])
                        p.dma(sync, "qZW", out=wzl[:, :], in_=io["sWzL"][ct])
                        zwm[wpar] = p.mark("qZW")

                    @b.tensor
                    def _(t, ct=ct, wpar=wpar):
                        p.wait_at(t, "qZW", zwm[wpar])
                        p.wait_at(t, "sDV", xsplit[0])
                        p.wait_at(t, "sAC", zpsa[0])
                        ins = None
                        for tc in range(4):
                            for kk in range(NKK):
                                sl = slice(kk * 512 + tc * 128,
                                           kk * 512 + (tc + 1) * 128)
                                ws_ = slice(kk * 512, (kk + 1) * 512)
                                psl = slice(tc * 512, (tc + 1) * 512)
                                t.matmul(pz[:, psl], xbh[:, sl], wzh[:, ws_],
                                         start=(kk == 0), stop=False)
                                t.matmul(pz[:, psl], xbh[:, sl], wzl[:, ws_],
                                         start=False, stop=False)
                                ins = t.matmul(pz[:, psl], xbl[:, sl],
                                               wzh[:, ws_], start=False,
                                               stop=(kk == NKK - 1))
                        p.inc(ins, "sPE")
                    zmmark = p.mark("sPE")
                    zwu[wpar] = zmmark
                    xuser[0] = zmmark

                    @b.scalar
                    def _(sc, zmmark=zmmark):
                        p.wait_at(sc, "sPE", zmmark)
                        p.wait_at(sc, "qZE0", zspill[0])
                        p.inc(sc.activation(zev2[:, :], pz[:, :], AF.Silu),
                              "sAC")
                    zpsa[0] = p.mark("sAC")

                    @b.gpsimd
                    def _(g, tb=tb, ct=ct):
                        p.wait_at(g, "sAC", zpsa[0])
                        p.dma(g, "qZE0",
                              out=io["ssz"][tb * 4:(tb + 1) * 4, :,
                                            ct * 512:(ct + 1) * 512].rearrange(
                                  "j p c -> p j c"),
                              in_=zev2[:, :].rearrange("p (j c) -> p j c", j=4))
                    zspill[0] = p.mark("qZE0")

            for cc in range(16, 34 if mini else 35):
                ccv = cc - 16
                wpar = cc % 2

                @b.sync
                def _(sync, cc=cc, wpar=wpar):
                    p.wait_at(sync, "sPE", fwu[wpar])
                    p.wait(sync, "qs1f")
                    p.dma(sync, "qW%d" % wpar,
                          out=wsh[wpar][:, :], in_=io["sWfH"][cc - 16])
                    p.dma(sync, "qW%d" % wpar,
                          out=wsl[wpar][:, :], in_=io["sWfL"][cc - 16])
                    wmark[wpar] = p.mark("qW%d" % wpar)

                ppar = ccv % 2

                @b.tensor
                def _(t, cc=cc, wpar=wpar, ppar=ppar, nt=nt):
                    p.wait_at(t, "qW%d" % wpar, wmark[wpar])
                    p.wait_at(t, "sDV", xsplit[0])
                    p.wait_at(t, "sDV", pfev[ppar])
                    ins = None
                    for kk in range(NKK):
                        ws_ = slice(kk * 128, (kk + 1) * 128)
                        xs_ = slice(kk * 512, kk * 512 + nt)
                        t.matmul(pf[ppar][:, 0:nt], wsh[wpar][:, ws_],
                                 xbh[:, xs_], start=(kk == 0), stop=False)
                        t.matmul(pf[ppar][:, 0:nt], wsh[wpar][:, ws_],
                                 xbl[:, xs_], start=False, stop=False)
                        ins = t.matmul(pf[ppar][:, 0:nt], wsl[wpar][:, ws_],
                                       xbh[:, xs_], start=False,
                                       stop=(kk == NKK - 1))
                    p.inc(ins, "sPE")
                fmm = p.mark("sPE")
                fwu[wpar] = fmm
                xuser[0] = fmm

                @b.vector
                def _(v, cc=cc, ccv=ccv, ppar=ppar, fmm=fmm, cur=cur, tb=tb,
                      nt=nt):
                    p.wait_at(v, "sPE", fmm)
                    p.wait(v, "sDV")
                    if cc == 34:
                        p.inc(v.tensor_copy(
                            dtacc[:, tb * 512:(tb + 1) * 512],
                            pf[ppar][0:32, :]), "sDV")
                        dtaccm[0] = p.mark("sDV")
                    else:
                        p.inc(v.tensor_copy(eap(cur, ccv)[:, 3:3 + nt],
                                            pf[ppar][:, 0:nt]), "sDV")
                pfev[ppar] = p.mark("sDV")


        if tb >= 1:
            ptb = tb - 1
            pown = ptb < 4
            for ccv in range(18):
                have_next = True

                @b.vector
                def _(v, ccv=ccv, cur=cur, prev=prev, have_next=have_next):
                    p.wait(v, "sDV")
                    if have_next:
                        v.tensor_copy(eap(prev, ccv)[:, 515:518],
                                      eap(cur, ccv)[:, 3:6])
                        p.wait(v, "sDV")
                        p.inc(v.tensor_copy(eap(cur, ccv)[:, 0:3],
                                            eap(prev, ccv)[:, 512:515]), "sDV")
                    else:
                        p.inc(v.memset(eap(prev, ccv)[:, 515:518], 0.0), "sDV")

                for dirB in (True,):
                    w = C["cwB" if dirB else "cwA"]
                    cb = C["cbB" if dirB else "cbA"]
                    offs = (6, 5, 4, 3) if dirB else (0, 1, 2, 3)
                    cvt = cvtB if dirB else cvtA
                    smarks = sBm if dirB else sAm
                    par = cvj % 2
                    cvj += 1

                    @b.vector
                    def _(v, prev=prev, offs=offs, w=w, ccv=ccv, cvt=cvt,
                          dirB=dirB):
                        p.wait(v, "sDV")
                        p.wait_at(v, "sAC", cvt_free[int(dirB)])
                        p.inc(v.tensor_scalar(
                            cvt[:, :],
                            eap(prev, ccv)[:, offs[0]:offs[0] + 512],
                            w[:, ccv * 4:ccv * 4 + 1], None, OP.mult),
                            "sDV")
                        for k in (1, 2, 3):
                            p.wait(v, "sDV")
                            p.inc(v.scalar_tensor_tensor(
                                cvt[:, :],
                                eap(prev, ccv)[:, offs[k]:offs[k] + 512],
                                w[:, ccv * 4 + k:ccv * 4 + k + 1],
                                cvt[:, :], OP.mult, OP.add), "sDV")
                    cvm = p.mark("sDV")

                    @b.scalar
                    def _(sc, cb=cb, ccv=ccv, par=par, cvm=cvm, cvt=cvt,
                          dirB=dirB):
                        p.wait_at(sc, "sDV", cvm)
                        p.wait_at(sc, "qCV%d" % par, cvsp[par])
                        p.inc(sc.activation(cvo[par][:, :], cvt[:, :], AF.Silu,
                                            bias=cb[:, ccv:ccv + 1]), "sAC")
                    smarks[ccv] = p.mark("sAC")
                    cvt_free[int(dirB)] = smarks[ccv]

                    if ccv < 16:
                        dname = "sxsB" if dirB else "sxsA"
                        row0 = ccv * 128
                    elif ccv == 16:
                        dname = "sBB" if dirB else "sBA"
                        row0 = 0
                    else:
                        dname = "sCB" if dirB else "sCA"
                        row0 = 0

                    @b.gpsimd
                    def _(g, dname=dname, row0=row0, ptb=ptb, par=par,
                          sm=smarks[ccv], ccv=ccv):
                        p.wait_at(g, "sAC", sm)
                        if ccv < 16:
                            dst = io[dname][ptb * 4:(ptb + 1) * 4, :,
                                            ccv * 128:(ccv + 1) * 128]
                        else:
                            dst = io[dname][ptb * 4:(ptb + 1) * 4,
                                            row0:row0 + 128, :]
                        p.dma(g, "qCV%d" % par,
                            out=dst.rearrange("j p t -> p j t"),
                            in_=cvo[par][:, :].rearrange("p (j t) -> p j t",
                                                         j=4))
                    cvsp[par] = p.mark("qCV%d" % par)

        if tb < 4:
            for ccv in range(18):
                w = C["cwA"]
                cb = C["cbA"]
                cvt = cvtA
                par = cvj % 2
                cvj += 1

                @b.vector
                def _(v, cur=cur, w=w, ccv=ccv, cvt=cvt):
                    p.wait(v, "sDV")
                    p.wait_at(v, "sAC", cvt_free[0])
                    p.inc(v.tensor_scalar(
                        cvt[:, :],
                        eap(cur, ccv)[:, 0:512],
                        w[:, ccv * 4:ccv * 4 + 1], None, OP.mult),
                        "sDV")
                    for k in (1, 2, 3):
                        p.wait(v, "sDV")
                        p.inc(v.scalar_tensor_tensor(
                            cvt[:, :],
                            eap(cur, ccv)[:, k:k + 512],
                            w[:, ccv * 4 + k:ccv * 4 + k + 1],
                            cvt[:, :], OP.mult, OP.add), "sDV")
                cvm = p.mark("sDV")

                @b.scalar
                def _(sc, cb=cb, ccv=ccv, par=par, cvm=cvm, cvt=cvt):
                    p.wait_at(sc, "sDV", cvm)
                    p.wait_at(sc, "qCV%d" % par, cvsp[par])
                    p.inc(sc.activation(cvo[par][:, :], cvt[:, :], AF.Silu,
                                        bias=cb[:, ccv:ccv + 1]), "sAC")
                sAm[ccv] = p.mark("sAC")
                cvt_free[0] = sAm[ccv]

                if ccv < 16:
                    dname = "sxsA"
                    row0 = ccv * 128
                elif ccv == 16:
                    dname = "sBA"
                    row0 = 0
                else:
                    dname = "sCA"
                    row0 = 0

                @b.gpsimd
                def _(g, dname=dname, row0=row0, ptb=tb, par=par,
                      sm=sAm[ccv], ccv=ccv):
                    p.wait_at(g, "sAC", sm)
                    if ccv < 16:
                        dst = io[dname][ptb * 4:(ptb + 1) * 4, :,
                                        ccv * 128:(ccv + 1) * 128]
                    else:
                        dst = io[dname][ptb * 4:(ptb + 1) * 4,
                                        row0:row0 + 128, :]
                    p.dma(g, "qCV%d" % par,
                        out=dst.rearrange("j p t -> p j t"),
                        in_=cvo[par][:, :].rearrange("p (j t) -> p j t",
                                                     j=4))
                cvsp[par] = p.mark("qCV%d" % par)

    dte = ring[0][0:32, 0:TOWN]
    for dd, dtb, sname in (("A", "dtbA", "sdtA"), ("B", "dtbB", "sdtB")):
        @b.scalar
        def _(sc, dtb=dtb):
            p.wait(sc, "sDV")
            p.wait(sc, "qDT0")
            p.inc(sc.activation(dte[:, :], dtacc[:, :], AF.Exp,
                                bias=C[dtb][:, 0:1]), "sAC")
            sc.wait_ge(p.sems["sAC"], p.ct["sAC"])
            p.inc(sc.activation(dte[:, :], dte[:, :], AF.Ln, bias=1.0),
                  "sAC")

        @b.gpsimd
        def _(g, sname=sname):
            p.wait(g, "sAC")
            p.dma(g, "qDT0", out=io[sname][:, 0:TOWN], in_=dte[:, :])


def emit_exchange(p, io, C, st, srunA, srunB):
    b = p.block
    QX = p.add_sem("qxcL")
    g0 = p.sb("xc_g0", [DST, DIN], stack=st)
    g1 = p.sb("xc_g1", [DST, DIN], stack=st)
    fselt = p.sb("xc_fsel", [DST, 1], stack=st)

    @b.sync
    def _(sync):
        p.wait_everything(sync)
        p.dma(sync, QX, out=io["cgin"][:, :], in_=srunA[:, :])
        p.dma(sync, QX, out=fselt[:, :], in_=io["fsel"][:, :])

    @b.gpsimd
    def _(g):
        p.wait(g, QX)
        p.inc(g.collective_compute(
            "AllGather", mybir.AluOpType.bypass,
            ins=[io["cgin"][:, :]], outs=[io["cgout"][:, :, :]],
            replica_groups=[[0, 1], [2, 3], [4, 5], [6, 7]]), "sGP")

    @b.sync
    def _(sync):
        p.wait(sync, "sGP")
        p.dma(sync, QX, out=g0[:, :], in_=io["cgout"][0])
        p.dma(sync, QX, out=g1[:, :], in_=io["cgout"][1])

    @b.vector
    def _(v):
        p.wait(v, QX)
        p.inc(v.tensor_tensor(g1[:, :], g1[:, :], g0[:, :], OP.subtract),
              "sDV")
        p.wait(v, "sDV")
        p.inc(v.scalar_tensor_tensor(srunB[:, :], g1[:, :], fselt[:, 0:1],
                                     g0[:, :], OP.mult, OP.add), "sDV")


def gen_ssd(p, io, C, st, dirB, chunks, compute_y, srun, cont, tag, combine,
            spill_log=None, tiles_out=None):
    b = p.block
    t = tag
    A_t = C["AB" if dirB else "AA"]
    mask = C["maskB" if dirB else "maskA"]
    dgD = C["dgDB" if dirB else "dgDA"]
    nwt = C["nwBt" if dirB else "nwAt"]
    sxs, sB, sC = ("sxsB", "sBB", "sCB") if dirB else ("sxsA", "sBA", "sCA")
    sdt = "sdtB" if dirB else "sdtA"
    SP, SD, SA, SG = (p.add_sem(t + x) for x in ("PE", "DV", "AC", "GP"))
    QL = p.add_sem("q" + t + "L")
    QG = p.add_sem("q" + t + "G")
    QG1 = p.add_sem("q" + t + "G1")
    QE = p.add_sem("q" + t + "E")
    QC = p.add_sem("q" + t + "C")
    QF = [p.add_sem("q" + t + "F%d" % i) for i in range(3)]
    QK = [p.add_sem("q" + t + "K%d" % i) for i in range(2)]
    QZ = p.add_sem("q" + t + "Z")

    cbase = min(chunks)
    tsel = 0 if dirB else 127

    crow = p.sb(t + "_crow", [128, NH * 128], stack=st)
    dtall = crow[0:32, 0:16 * Q]
    acall = dtall
    stmp = p.sb(t + "_stmp", [128, 1024], stack=st)
    ttm = p.sb(t + "_ttm", [128, 1024], stack=st)
    actm = p.sb(t + "_actm", [128, 512], stack=st)
    asall = p.sb(t + "_asall", [128, 512], stack=st)
    easall = asall
    ddtm = p.sb(t + "_ddtm", [128, 512], stack=st)
    if compute_y:
        eactm = p.sb(t + "_eactm", [128, 512], stack=st)
    assert compute_y
    xsckP = [p.sb(t + "_xsck%d" % i, [128, 16 * 128], stack=st)
             for i in range(3)]
    bckP = [p.sb(t + "_bck%d" % i, [128, 128], stack=st) for i in range(3)]
    xdP = [p.sb(t + "_xd%d" % i, [128, DIN], stack=st) for i in range(2)]
    xddP = [p.sb(t + "_xdd%d" % i, [128, DIN], stack=st) for i in range(2)]
    btmP = [p.sb(t + "_btm%d" % i, [128, 128], stack=st) for i in range(2)]
    cckP = [p.sb(t + "_cck%d" % i, [128, 128], stack=st) for i in range(2)]
    mallP = [p.sb(t + "_mall%d" % i, [128, NH * 128], stack=st)
             for i in range(2)]
    gm = p.sb(t + "_gm", [128, 128], stack=st)
    ysb = p.sb(t + "_ysb", [128, DIN], stack=st)
    szck = p.sb(t + "_szck", [128, DIN], stack=st)
    ssq = p.sb(t + "_ssq", [128, 4], stack=st)
    pX = p.ps(t + "_pX", [128, 1024], stack=st)
    pH1 = p.ps(t + "_pH1", [128, 1024], stack=st)
    pH2 = p.ps(t + "_pH2", [128, 1024], stack=st)
    if tiles_out is not None:
        tiles_out.update(mallP=mallP, crow=crow, xsckP=xsckP, xdP=xdP,
                         xddP=xddP, ysb=ysb, szck=szck, stmp=stmp, ttm=ttm,
                         actm=actm, asall=asall, ddtm=ddtm, eactm=eactm,
                         bckP=bckP, pX=pX, pH1=pH1)

    @b.vector
    def _(v):
        p.wait(v, "sPE", "sDV", "sAC", "sGP")
        p.wait(v, SP, SD, SA)
        p.wait_all_dma(v)
        if not cont:
            p.inc(v.memset(srun[:, :], 0.0), SD)
        else:
            p.inc(v.tensor_copy(srun[0:1, 0:1], srun[0:1, 0:1]), SD)

    pfm = [0, 0, 0]
    Ym, Sm = {}, {}
    cckm, crm, msubm, gmm, gmevm = {}, {}, {}, {}, {}
    mkXe, gatem, nwm, ysbevm, szm, tailv = {}, {}, {}, {}, {}, {}
    srunm = {}

    def prefetch(sync, j):
        p3 = j % 3
        c = chunks[j]
        p.wait_at(sync, SP, Ym.get(j - 3, 0))
        p.dma(sync, QF[p3], out=xsckP[p3][:, :], in_=io[sxs][c])
        p.dma(sync, QF[p3], out=bckP[p3][:, :], in_=io[sB][c])
        pfm[p3] = p.mark(QF[p3])

    @b.sync
    def _(sync):
        p.wait_everything(sync)
        p.dma(sync, QL, out=dtall[:, :],
              in_=io[sdt][:, cbase * Q:(cbase + 16) * Q])
        prefetch(sync, 0)
    yield

    @b.tensor
    def _(t_):
        p.wait(t_, QL, SD)
        ins = None
        for k in range(16):
            ins = t_.transpose(pX[:, k * 32:(k + 1) * 32],
                               dtall[:, k * Q:(k + 1) * Q],
                               C["ident"][0:32, 0:32])
        p.inc(ins, SP)

    @b.vector
    def _(v):
        p.wait(v, SP)
        v.tensor_copy(ttm[:, 0:512], pX[:, 0:512])
        p.inc(v.tensor_scalar(dtall[:, :], dtall[:, :], A_t[:, 0:1], None,
                              OP.mult), SD)
    yield

    @b.tensor
    def _(t_):
        p.wait(t_, SD)
        ins = None
        for k in range(16):
            ins = t_.transpose(pX[:, 512 + k * 32:512 + (k + 1) * 32],
                               dtall[:, k * Q:(k + 1) * Q],
                               C["ident"][0:32, 0:32])
        p.inc(ins, SP)

    @b.vector
    def _(v):
        p.wait(v, SP)
        p.inc(v.tensor_copy(ttm[:, 512:1024], pX[:, 512:1024]), SD)

    @b.tensor
    def _(t_):
        p.wait(t_, SD)
        ins = None
        for k in range(16):
            ins = t_.matmul(pX[:, k * 32:(k + 1) * 32], mask[:, :],
                            ttm[:, 512 + k * 32:512 + (k + 1) * 32],
                            start=True, stop=True)
        p.inc(ins, SP)
    yield

    @b.vector
    def _(v):
        p.wait(v, SP)
        p.inc(v.tensor_copy(actm[:, :], pX[:, 0:512]), SD)

    if compute_y:
        @b.scalar
        def _(sc):
            p.wait(sc, SD)
            p.inc(sc.activation(eactm[:, :], actm[:, :], AF.Exp), SA)

        for half in range(2):
            @b.tensor
            def _(t_, half=half):
                p.wait(t_, SD)
                ins = None
                for k in range(8):
                    kk = half * 8 + k
                    ins = t_.matmul(pX[0:32, k * 128:(k + 1) * 128],
                                    ttm[:, 512 + kk * 32:512 + (kk + 1) * 32],
                                    mask[:, :], start=True, stop=True)
                p.inc(ins, SP)

            @b.vector
            def _(v, half=half):
                p.wait(v, SP)
                p.inc(v.tensor_copy(acall[:, half * 1024:(half + 1) * 1024],
                                    pX[0:32, 0:1024]), SD)
        yield

    @b.sync
    def _(sync):
        p.wait(sync, SD)
        if compute_y:
            p.wait(sync, SA)
            for k in range(16):
                p.dma(sync, QL,
                      out=io["srowC" + t][0:1, k * 4096:(k + 1) * 4096]
                      .rearrange("o (h tt) -> o h tt", h=NH),
                      in_=acall[:, k * 128:(k + 1) * 128].unsqueeze(1))
        p.dma(sync, QL, out=io["srowS" + t][:, :],
              in_=actm[tsel:tsel + 1, :])
        p.wait(sync, QL)
        p.dma(sync, QE, out=asall[:, :],
              in_=io["srowS" + t][0:1, :].partition_broadcast(128).squeeze(1))

    @b.vector
    def _(v):
        p.wait(v, QE)
        p.inc(v.tensor_tensor(ddtm[:, :], asall[:, :], actm[:, :],
                              OP.subtract), SD)

    @b.scalar
    def _(sc):
        p.wait(sc, SD)
        sc.activation(easall[:, :], asall[:, :], AF.Exp)
        p.inc(sc.activation(ddtm[:, :], ddtm[:, :], AF.Exp), SA)

    @b.vector
    def _(v):
        p.wait(v, SA)
        p.inc(v.tensor_tensor(ddtm[:, :], ddtm[:, :], ttm[:, 0:512],
                              OP.mult), SD)
    yield

    prep_sd = p.mark(SD)
    prep_sa = p.mark(SA)
    n = len(chunks)

    def emit_prep(ci):
        c = chunks[ci]
        kk2 = c - cbase
        p2 = ci % 2
        p3 = ci % 3

        @b.sync
        def _(sync, ci=ci, c=c, kk2=kk2, p2=p2):
            p.wait_at(sync, SP, Ym.get(ci - 2, 0))
            p.dma(sync, QK[p2], out=cckP[p2][:, :], in_=io[sC][c])
            cckm[ci] = p.mark(QK[p2])
            p.wait_at(sync, SD, msubm.get(ci - 1, 0))
            p.dma(sync, QC, out=crow[:, :],
                  in_=io["srowC" + t][0:1, kk2 * 4096:(kk2 + 1) * 4096]
                  .partition_broadcast(128).squeeze(1))
            crm[ci] = p.mark(QC)
            if ci + 1 < n:
                prefetch(sync, ci + 1)

        xevm = gmevm.get(ci - 1, prep_sd)
        for hf in range(2):
            @b.tensor
            def _(t_, ci=ci, hf=hf, p3=p3, xevm=xevm):
                p.wait_at(t_, QF[p3], pfm[p3])
                p.wait_at(t_, SD, xevm)
                ins = None
                for i in range(8):
                    ins = t_.transpose(pX[:, i * 128:(i + 1) * 128],
                                       xsckP[p3][:, (hf * 8 + i) * 128:
                                                 (hf * 8 + i + 1) * 128],
                                       C["ident"][:, :])
                p.inc(ins, SP)
            xtm = p.mark(SP)

            @b.vector
            def _(v, hf=hf, ci=ci, kk2=kk2, p2=p2, xtm=xtm):
                p.wait_at(v, SP, xtm)
                p.wait_at(v, SP, Sm.get(ci - 2, 0))
                p.wait_at(v, SD, gatem.get(ci - 2, 0))
                sl = slice(hf * 1024, (hf + 1) * 1024)
                v.tensor_tensor(
                    xdP[p2][:, sl].rearrange("p (h e) -> p h e", h=16),
                    pX[:, :].rearrange("p (h e) -> p h e", h=16),
                    hb(ttm[:, kk2 * 32 + 16 * hf:kk2 * 32 + 16 * (hf + 1)]),
                    OP.mult)
                p.inc(v.tensor_tensor(
                    xddP[p2][:, sl].rearrange("p (h e) -> p h e", h=16),
                    pX[:, :].rearrange("p (h e) -> p h e", h=16),
                    hb(ddtm[:, kk2 * 32 + 16 * hf:kk2 * 32 + 16 * (hf + 1)]),
                    OP.mult), SD)
            xevm = p.mark(SD)
        mkXe[ci] = p.mark(SD)

        @b.tensor
        def _(t_, ci=ci, p2=p2, p3=p3):
            p.wait_at(t_, SD, mkXe[ci])
            t_.transpose(pX[:, 0:128], bckP[p3][:, :], C["ident"][:, :])
            p.wait_at(t_, QK[p2], cckm[ci])
            p.inc(t_.matmul(pX[:, 128:256], bckP[p3][:, :], cckP[p2][:, :],
                            start=True, stop=True), SP)
        bgm = p.mark(SP)

        @b.vector
        def _(v, ci=ci, p2=p2, bgm=bgm):
            p.wait_at(v, SP, bgm)
            p.wait_at(v, SP, Sm.get(ci - 2, 0))
            v.tensor_copy(btmP[p2][:, :], pX[:, 0:128])
            p.inc(v.tensor_tensor(gm[:, :], pX[:, 128:256], mask[:, :],
                                  OP.mult), SD)
        gmevm[ci] = p.mark(SD)

        @b.vector
        def _(v, ci=ci, kk2=kk2, p2=p2):
            p.wait_at(v, QC, crm[ci])
            p.wait_at(v, SP, Ym.get(ci - 2, 0))
            p.inc(v.tensor_tensor(
                mallP[p2][:, :].rearrange("p (h l) -> p h l", h=NH),
                crow[:, :].rearrange("p (h l) -> p h l", h=NH),
                actm[:, kk2 * 32:(kk2 + 1) * 32].unsqueeze(2)
                .broadcast_to([128, NH, 128]),
                OP.subtract), SD)
            p.wait(v, SD)
            p.inc(v.tensor_scalar(mallP[p2][:, :], mallP[p2][:, :], 30.0,
                                  None, OP.min), SD)
        msubm[ci] = p.mark(SD)

        @b.scalar
        def _(sc, ci=ci, p2=p2):
            p.wait_at(sc, SD, msubm[ci])
            p.inc(sc.activation(mallP[p2][:, :], mallP[p2][:, :], AF.Exp),
                  SA)
        mexm = p.mark(SA)

        @b.vector
        def _(v, p2=p2, mexm=mexm):
            p.wait_at(v, SA, mexm)
            p.inc(v.tensor_tensor(
                mallP[p2][:, :].rearrange("p (h l) -> p h l", h=NH),
                mallP[p2][:, :].rearrange("p (h l) -> p h l", h=NH),
                gm[:, :].unsqueeze(1).broadcast_to([128, NH, 128]),
                OP.mult), SD)
        gmm[ci] = p.mark(SD)

    def emit_tail(ci):
        c = chunks[ci]
        kk2 = c - cbase
        p2 = ci % 2
        p3 = ci % 3

        @b.sync
        def _(sync, ci=ci, c=c):
            p.wait_at(sync, SD, gatem.get(ci - 1, 0))
            p.dma(sync, QZ, out=szck[:, :], in_=io["ssz"][c])
        szm[ci] = p.mark(QZ)

        evm0 = 0
        for hf in range(2):
            @b.tensor
            def _(t_, ci=ci, hf=hf, p2=p2, p3=p3, evm0=evm0):
                p.wait_at(t_, SD, max(gmm[ci], tailv.get(ci - 1, 0), evm0))
                for j in range(2):
                    t_.matmul(pH1[:, j * 512:(j + 1) * 512], cckP[p2][:, :],
                              srun[:, hf * 1024 + j * 512:
                                   hf * 1024 + (j + 1) * 512],
                              start=True, stop=True)
                for i in range(8):
                    t_.matmul(pH2[:, i * 128:(i + 1) * 128],
                              xsckP[p3][:, (hf * 8 + i) * 128:
                                        (hf * 8 + i + 1) * 128],
                              dgD[:, (hf * 8 + i) * 128:
                                  (hf * 8 + i + 1) * 128],
                              start=(i % 4 == 0), stop=False)
                ins = None
                for h in range(16):
                    hh = hf * 16 + h
                    ins = t_.matmul(pH2[:, h * HD:(h + 1) * HD],
                                    mallP[p2][:, hh * 128:(hh + 1) * 128],
                                    xdP[p2][:, hh * HD:(hh + 1) * HD],
                                    start=False, stop=(h % 8 == 7))
                p.inc(ins, SP)
            ym = p.mark(SP)

            @b.vector
            def _(v, hf=hf, ci=ci, kk2=kk2, ym=ym):
                p.wait_at(v, SP, ym)
                if hf == 0:
                    p.wait(v, QG, QG1)
                sl = slice(hf * 1024, (hf + 1) * 1024)
                p.inc(v.tensor_tensor(
                    ysb[:, sl].rearrange("p (h e) -> p h e", h=16),
                    pH1[:, :].rearrange("p (h e) -> p h e", h=16),
                    hb(eactm[:, kk2 * 32 + 16 * hf:
                             kk2 * 32 + 16 * (hf + 1)]), OP.mult), SD)
                p.wait(v, SD)
                p.inc(v.tensor_tensor(ysb[:, sl], ysb[:, sl], pH2[:, :],
                                      OP.add), SD)
            evm0 = p.mark(SD)
        Ym[ci] = p.mark(SP)
        ysbevm[ci] = p.mark(SD)

        @b.vector
        def _(v, ci=ci, p2=p2):
            p.wait_at(v, QZ, szm[ci])
            p.wait(v, SD)
            p.inc(v.tensor_tensor(ysb[:, :], ysb[:, :], szck[:, :],
                                  OP.mult), SD)
            p.wait(v, SD)
            p.inc(v.scalar_tensor_tensor(xdP[p2][:, :], ysb[:, :], 1.0,
                                         ysb[:, :], OP.mult, OP.mult,
                                         accum_out=ssq[:, 0:1]), SD)
        gatem[ci] = p.mark(SD)

        @b.scalar
        def _(sc, ci=ci):
            p.wait_at(sc, SD, gatem[ci])
            p.inc(sc.activation(ssq[:, 1:2], ssq[:, 0:1], AF.Ln,
                                scale=1.0 / DIN, bias=C["epsb"][:, 0:1]),
                  SA)
            sc.wait_ge(p.sems[SA], p.ct[SA])
            p.inc(sc.activation(ssq[:, 2:3], ssq[:, 1:2], AF.Exp,
                                scale=-0.5), SA)
        rmsm = p.mark(SA)

        @b.vector
        def _(v, rmsm=rmsm):
            p.wait_at(v, SA, rmsm)
            p.inc(v.scalar_tensor_tensor(ysb[:, :], ysb[:, :], ssq[:, 2:3],
                                         nwt[:, :], OP.mult, OP.mult), SD)
        nwm[ci] = p.mark(SD)

        srh = ysbevm[ci]
        addm = srunm.get(ci - 1, 0)
        for hf in range(2):
            @b.tensor
            def _(t_, ci=ci, hf=hf, p2=p2, srh=srh):
                p.wait_at(t_, SD, srh)
                ins = None
                for j in range(2):
                    ins = t_.matmul(pH1[:, j * 512:(j + 1) * 512],
                                    btmP[p2][:, :],
                                    xddP[p2][:, hf * 1024 + j * 512:
                                             hf * 1024 + (j + 1) * 512],
                                    start=True, stop=True)
                p.inc(ins, SP)
            stm = p.mark(SP)

            @b.gpsimd
            def _(g, hf=hf, kk2=kk2, addm=addm):
                p.wait_at(g, SD, addm)
                p.wait_at(g, SA, prep_sa)
                p.wait(g, QE)
                sl = slice(hf * 1024, (hf + 1) * 1024)
                p.inc(g.tensor_tensor(
                    stmp[:, :].rearrange("p (h e) -> p h e", h=16),
                    srun[:, sl].rearrange("p (h e) -> p h e", h=16),
                    hb(easall[:, kk2 * 32 + 16 * hf:kk2 * 32 + 16 * (hf + 1)]),
                    OP.mult), SG)
            multm = p.mark(SG)

            @b.vector
            def _(v, hf=hf, stm=stm, multm=multm):
                p.wait_at(v, SP, stm)
                p.wait_at(v, SG, multm)
                sl = slice(hf * 1024, (hf + 1) * 1024)
                p.inc(v.tensor_tensor(srun[:, sl], pH1[:, :], stmp[:, :],
                                      OP.add), SD)
            addm = p.mark(SD)
            srh = addm
        srunm[ci] = addm
        Sm[ci] = p.mark(SP)
        tailv[ci] = p.mark(SD)

        qg = (QG, QG1)[len(spill_log) % 2] if spill_log is not None else QG

        @b.gpsimd
        def _(g, c=c, ci=ci, combine=combine, qg=qg):
            p.wait_at(g, SD, nwm[ci])
            if combine:
                p.dma(g, qg, out=io["sYS"][c], in_=ysb[:, :],
                      accum_op=OP.add)
            else:
                p.dma(g, qg, out=io["sYS"][c], in_=ysb[:, :])
        if spill_log is not None:
            spill_log.append((qg, p.mark(qg)))

    emit_prep(0)
    yield
    emit_prep(1)
    yield
    for i in range(n):
        emit_tail(i)
        yield "c"
        if i + 2 < n:
            emit_prep(i + 2)
            yield


def gen_s5b(p, io, C, st, spill_log, xlog):
    b = p.block
    SP, SD, SA = (p.add_sem("sb" + x) for x in ("PE", "DV", "AC"))
    QL = p.add_sem("qsbL")
    QWP = [p.add_sem("qsbW%d" % i) for i in range(2)]
    QG = p.add_sem("qsbG")
    QG1 = p.add_sem("qsbG1")
    woutP = [p.sb("o_wout%d" % i, [128, 2 * 512], stack=st)
             for i in range(2)]
    ysum = p.sb("o_ysum", [128, DIN], stack=st)
    yTf = ysum
    xres = ysum[:, 0:DM]
    dmp = ysum[:, DM:2 * DM]
    xtm = ysum[:, DM:2 * DM]
    sm = p.sb("o_sm", [128, 8], stack=st)
    pO = p.ps("o_pO", [128, 1024], stack=st)

    @b.sync
    def _(sync):
        p.wait(sync, "sPE", "sDV", "sAC", "sGP")
        p.wait_all_dma(sync)
        sync.nop()
    yield

    for i, c in enumerate(range(OCK - 1, -1, -1)):
        while len(spill_log) <= i:
            yield

        @b.sync
        def _(sync, c=c, i=i):
            p.wait(sync, SD, QG, QG1)
            sem, mk = spill_log[i]
            p.wait_at(sync, sem, mk)
            p.dma(sync, QL, out=ysum[:, :], in_=io["sYS"][c])
        yield

        for hf in range(2):
            @b.tensor
            def _(t_, hf=hf):
                p.wait(t_, SD, QL)
                ins = None
                for i in range(8):
                    ins = t_.transpose(pO[:, i * 128:(i + 1) * 128],
                                       ysum[:, (hf * 8 + i) * 128:
                                            (hf * 8 + i + 1) * 128],
                                       C["ident"][:, :])
                p.inc(ins, SP)

            @b.vector
            def _(v, hf=hf):
                p.wait(v, SP)
                p.inc(v.tensor_copy(yTf[:, hf * 1024:(hf + 1) * 1024],
                                    pO[:, :]), SD)
        yield

        wpm = [0, 0]
        for q in range(16):
            dmt, kq = q // 8, q % 8
            wp = q % 2

            @b.sync
            def _(sync, dmt=dmt, kq=kq, wp=wp):
                p.wait_at(sync, SP, wpm[wp])
                p.dma(sync, QWP[wp], out=woutP[wp][:, :],
                      in_=io["WoutT2"][dmt][:, kq * 1024:(kq + 1) * 1024])
            wm = p.mark(QWP[wp])

            @b.tensor
            def _(t_, dmt=dmt, kq=kq, wp=wp, wm=wm):
                p.wait_at(t_, QWP[wp], wm)
                p.wait(t_, SD)
                ins = None
                for k in range(2):
                    kk = kq * 2 + k
                    ins = t_.matmul(pO[:, dmt * 512:(dmt + 1) * 512],
                                    yTf[:, kk * 128:(kk + 1) * 128],
                                    woutP[wp][:, k * 512:(k + 1) * 512],
                                    start=(kk == 0), stop=(kk == 15))
                p.inc(ins, SP)
            wpm[wp] = p.mark(SP)

        @b.sync
        def _(sync, c=c):
            p.wait(sync, SP)
            p.dma(sync, QL, out=xtm[:, :],
                  in_=io["x_tm"][c * Q:(c + 1) * Q, :])
        yield

        @b.vector
        def _(v):
            p.wait(v, SP, QG, QG1)
            p.wait(v, QL)
            p.inc(v.tensor_tensor(xres, pO[:, 0:DM], xtm[:, :], OP.add),
                  SD)
            p.wait(v, SD)
            p.inc(v.tensor_scalar(dmp, xres, 0.0, 0.0, OP.add,
                                  OP.add, accum_out=sm[:, 0:1]), SD)
            p.wait(v, SD)
            p.inc(v.scalar_tensor_tensor(dmp, xres, 1.0,
                                         xres, OP.mult, OP.mult,
                                         accum_out=sm[:, 1:2]), SD)
            p.wait(v, SD)
            v.tensor_scalar(sm[:, 2:3], sm[:, 0:1], 1.0 / DM, None, OP.mult)
            p.inc(v.tensor_scalar(sm[:, 3:4], sm[:, 1:2], 1.0 / DM, None,
                                  OP.mult), SD)
            p.wait(v, SD)
            p.inc(v.tensor_tensor(sm[:, 4:5], sm[:, 2:3], sm[:, 2:3],
                                  OP.mult), SD)
            p.wait(v, SD)
            p.inc(v.tensor_tensor(sm[:, 5:6], sm[:, 3:4], sm[:, 4:5],
                                  OP.subtract), SD)

        @b.scalar
        def _(sc):
            p.wait(sc, SD)
            p.inc(sc.activation(sm[:, 6:7], sm[:, 5:6], AF.Ln,
                                bias=C["epsb"][:, 0:1]), SA)
            sc.wait_ge(p.sems[SA], p.ct[SA])
            p.inc(sc.activation(sm[:, 7:8], sm[:, 6:7], AF.Exp, scale=-0.5),
                  SA)

        @b.vector
        def _(v):
            p.wait(v, SA)
            p.inc(v.tensor_scalar(xres, xres, sm[:, 2:3],
                                  sm[:, 7:8], OP.subtract, OP.mult), SD)

        qx = (QG, QG1)[len(xlog) % 2]

        @b.gpsimd
        def _(g, c=c, qx=qx):
            p.wait(g, SD)
            p.dma(g, qx, out=io["sXLN"][c * Q:(c + 1) * Q, :], in_=xres)
        xlog.append((qx, p.mark(qx)))
        yield "c"


def gen_s6(p, io, C, T, xlog):
    b = p.block
    TT = 256
    u1 = T["ysb"]
    v1P = [T["mallP"][0], T["mallP"][1]]
    u2Q = [T["crow"][:, 0:2048], T["crow"][:, 2048:4096],
           T["xsckP"][0], T["xsckP"][1]]
    v2 = T["szck"]
    b2tP = [T["ddtm"], T["eactm"]]
    xlt = T["xsckP"][2]
    xlTP = [T["ttm"], T["stmp"]]
    pfm = T["actm"]
    qfm = T["asall"]
    hfmQ = [T["xdP"][0], T["xdP"][1], T["xddP"][0], T["xddP"][1]]
    o2 = T["ttm"]
    dmp = T["xdP"][0][:, 0:1024]
    sm = T["bckP"][0][:, 0:8]
    pAP = [T["pX"], T["pH1"]]

    def hf(ffc):
        return hfmQ[ffc // 8][:, (ffc % 8) * TT:(ffc % 8 + 1) * TT]

    def paT(c0, w):
        h = c0 // 1024
        return pAP[h][:, c0 - h * 1024:c0 - h * 1024 + w]

    @b.sync
    def _(sync):
        for nm in list(p.sems):
            if not nm.startswith("qsb"):
                p.wait(sync, nm)
        p.dma(sync, "qS",
            out=u1[:, :].rearrange("p (kk rc c) -> p kk rc c", kk=NKK, rc=2),
            in_=io["U1b"].rearrange("kk rc p c -> p kk rc c"))
        for rc in range(2):
            p.dma(sync, "qS",
                out=v1P[rc][:, :].rearrange("p (ff c) -> p ff c", ff=32),
                in_=io["V1b"][rc].rearrange("ff p c -> p ff c"))
        for q in range(4):
            p.dma(sync, "qS",
                out=u2Q[q][:, :].rearrange("p (ff rc c) -> p ff rc c",
                                           ff=8, rc=2),
                in_=io["U2b"][q * 8:(q + 1) * 8].rearrange(
                    "ff rc p c -> p ff rc c"))
        p.dma(sync, "qS",
            out=v2[:, :].rearrange("p (rc dm c) -> p rc dm c", rc=2, dm=8),
            in_=io["V2b"].rearrange("rc dm p c -> p rc dm c"))
    yield

    @b.sync
    def _(sync):
        p.wait(sync, "qS")
        for h in range(2):
            p.dma(sync, "qS",
                out=b2tP[h][:, :],
                in_=io["b2"][h * 512:(h + 1) * 512].unsqueeze(0)
                .partition_broadcast(128).squeeze(1))
    yield

    for tt in range(7, -1, -1):
        @b.sync
        def _(sync, tt=tt):
            p.wait(sync, "sPE", "sDV", "sAC", "qG")
            for idx in (15 - 2 * tt - 1, 15 - 2 * tt):
                sem, mk = xlog[idx]
                p.wait_at(sync, sem, mk)
            p.dma(sync, "qS",
                out=xlt[:, :].rearrange("p (tc d) -> p tc d", tc=2),
                in_=io["sXLN"][tt * TT:(tt + 1) * TT, :].rearrange(
                    "(tc p) d -> p tc d", p=128))
        yield

        @b.tensor
        def _(t_):
            p.wait(t_, "qS", "sDV")
            ins = None
            for kk in range(NKK):
                for tc in range(2):
                    j = kk * 2 + tc
                    ins = t_.transpose(
                        pAP[j // 8][:, (j % 8) * 128:(j % 8 + 1) * 128],
                        xlt[:, tc * DM + kk * 128:tc * DM + (kk + 1) * 128],
                        C["ident"][:, :])
            p.inc(ins, "sPE")

        @b.vector
        def _(v):
            p.wait(v, "sPE", "qG")
            for h in range(2):
                ins = v.tensor_copy(xlTP[h][:, :], pAP[h][:, :])
            p.inc(ins, "sDV")
        yield

        for rc in range(2):
            @b.tensor
            def _(t_, rc=rc):
                p.wait(t_, "sDV", "sAC")
                ins = None
                for kk in range(NKK):
                    ins = t_.matmul(paT(0, TT),
                                    u1[:, (kk * 2 + rc) * 128:
                                       (kk * 2 + rc + 1) * 128],
                                    xlTP[kk // 4][:, (kk % 4) * TT:
                                                  (kk % 4 + 1) * TT],
                                    start=(kk == 0), stop=(kk == NKK - 1))
                p.inc(ins, "sPE")

            @b.vector
            def _(v, rc=rc):
                p.wait(v, "sPE")
                p.inc(v.tensor_copy(pfm[:, rc * TT:(rc + 1) * TT],
                                    paT(0, TT)), "sDV")
        yield

        for fg in range(8):
            @b.tensor
            def _(t_, fg=fg):
                p.wait(t_, "sDV")
                if fg < 2:
                    p.wait(t_, "sAC")
                ins = None
                for j in range(4):
                    ffc = fg * 4 + j
                    for rc in range(2):
                        ins = t_.matmul(paT(512 + j * TT, TT),
                                        v1P[rc][:, ffc * 128:(ffc + 1) * 128],
                                        pfm[:, rc * TT:(rc + 1) * TT],
                                        start=(rc == 0), stop=(rc == 1))
                p.inc(ins, "sPE")
            fgm = p.mark("sPE")

            @b.vector
            def _(v, fg=fg, fgm=fgm):
                p.wait_at(v, "sPE", fgm)
                ins = None
                for j in range(4):
                    ffc = fg * 4 + j
                    ins = v.tensor_scalar(hf(ffc), paT(512 + j * TT, TT),
                                          C["b1r"][:, ffc:ffc + 1], None,
                                          OP.add)
                p.inc(ins, "sDV")
        yield

        @b.scalar
        def _(sc):
            p.wait(sc, "sDV")
            ins = None
            for q in range(4):
                ins = sc.activation(hfmQ[q][:, :], hfmQ[q][:, :],
                                    AF.Gelu_apprx_tanh)
            p.inc(ins, "sAC")

        for rc in range(2):
            @b.tensor
            def _(t_, rc=rc):
                p.wait(t_, "sAC", "sDV")
                ins = None
                for ffc in range(32):
                    ins = t_.matmul(paT(1024, TT),
                                    u2Q[ffc // 8][:, ((ffc % 8) * 2 + rc)
                                                  * 128:((ffc % 8) * 2 + rc
                                                         + 1) * 128],
                                    hf(ffc),
                                    start=(ffc == 0), stop=(ffc == 31))
                p.inc(ins, "sPE")

            @b.vector
            def _(v, rc=rc):
                p.wait(v, "sPE")
                p.inc(v.tensor_copy(qfm[:, rc * TT:(rc + 1) * TT],
                                    paT(1024, TT)), "sDV")
        yield

        for tc in range(2):
            @b.tensor
            def _(t_, tc=tc):
                p.wait(t_, "sDV", "sGP")
                ins = None
                for dmt in range(2):
                    for rc in range(2):
                        ins = t_.matmul(
                            paT(1024 + dmt * 512, 512),
                            qfm[:, rc * TT + tc * 128:rc * TT + (tc + 1) * 128],
                            v2[:, (rc * 8 + dmt * 4) * 128:
                               (rc * 8 + dmt * 4 + 4) * 128],
                            start=(rc == 0), stop=(rc == 1))
                p.inc(ins, "sPE")

            @b.vector
            def _(v, tc=tc):
                p.wait(v, "sPE", "qG")
                for h in range(2):
                    ins = v.tensor_tensor(o2[:, h * 512:(h + 1) * 512],
                                          pAP[1][:, h * 512:(h + 1) * 512],
                                          b2tP[h][:, :], OP.add)
                p.inc(ins, "sDV")
                p.wait(v, "sDV")
                p.inc(v.tensor_tensor(o2[:, :], o2[:, :],
                                      xlt[:, tc * DM:(tc + 1) * DM], OP.add),
                      "sDV")
                p.wait(v, "sDV")
                p.inc(v.tensor_scalar(dmp[:, :], o2[:, :], 0.0, 0.0, OP.add,
                                      OP.add, accum_out=sm[:, 0:1]), "sDV")
                p.wait(v, "sDV")
                p.inc(v.tensor_tensor(dmp[:, :], o2[:, :], o2[:, :], OP.mult),
                      "sDV")
                p.wait(v, "sDV")
                p.inc(v.tensor_scalar(dmp[:, :], dmp[:, :], 0.0, 0.0, OP.add,
                                      OP.add, accum_out=sm[:, 1:2]), "sDV")
                p.wait(v, "sDV")
                v.tensor_scalar(sm[:, 2:3], sm[:, 0:1], 1.0 / DM, None,
                                OP.mult)
                p.inc(v.tensor_scalar(sm[:, 3:4], sm[:, 1:2], 1.0 / DM, None,
                                      OP.mult), "sDV")
                p.wait(v, "sDV")
                p.inc(v.tensor_tensor(sm[:, 4:5], sm[:, 2:3], sm[:, 2:3],
                                      OP.mult), "sDV")
                p.wait(v, "sDV")
                p.inc(v.tensor_tensor(sm[:, 5:6], sm[:, 3:4], sm[:, 4:5],
                                      OP.subtract), "sDV")

            @b.scalar
            def _(sc):
                p.wait(sc, "sDV")
                p.inc(sc.activation(sm[:, 6:7], sm[:, 5:6], AF.Ln,
                                    bias=C["epsb"][:, 0:1]), "sAC")
                sc.wait_ge(p.sems["sAC"], p.ct["sAC"])
                p.inc(sc.activation(sm[:, 7:8], sm[:, 6:7], AF.Exp,
                                    scale=-0.5), "sAC")

            @b.vector
            def _(v):
                p.wait(v, "sAC")
                p.inc(v.tensor_scalar(o2[:, :], o2[:, :], sm[:, 2:3],
                                      sm[:, 7:8], OP.subtract, OP.mult),
                      "sDV")

            @b.gpsimd
            def _(g, tt=tt, tc=tc):
                p.wait(g, "sDV")
                p.dma(g, "qG",
                    out=io["out"][tt * TT + tc * 128:tt * TT + (tc + 1) * 128,
                                  :],
                    in_=o2[:, :])
            yield


def _blk(a, kk, cc):
    a = np.ascontiguousarray(a, dtype=np.float32)
    return np.ascontiguousarray(
        a.reshape(kk, 128, cc, 128).transpose(0, 2, 1, 3))


def prepare_inmaps(inputs):
    x = np.asarray(inputs["L0_attn_x"], dtype=np.float32)
    Win = np.asarray(inputs["L0_attn_W_in"], dtype=np.float32)
    Wout = np.asarray(inputs["L0_attn_W_out"], dtype=np.float32)
    WinTp = np.zeros((DM, DIPP), dtype=np.float32)
    WinTp[:, :4384] = Win.T
    WinT3 = np.ascontiguousarray(
        _blk(WinTp, NKK, NCC).transpose(1, 2, 0, 3)).reshape(NCC, 128, NKK * 128)
    WoutT2 = np.ascontiguousarray(
        Wout.T.reshape(16, 128, 2, 4, 128).transpose(2, 1, 0, 3, 4)
    ).reshape(2, 128, 8192)
    U1b = _blk(np.asarray(inputs["L0_ffn_U1"]), NKK, 2)
    V1b = _blk(np.asarray(inputs["L0_ffn_V1"]), 2, 32)
    U2b = _blk(np.asarray(inputs["L0_ffn_U2"]), 32, 2)
    V2b = _blk(np.asarray(inputs["L0_ffn_V2"]), 2, 8)
    wf = {k: np.ascontiguousarray(np.asarray(inputs["L0_attn_%s_f" % k]),
                                  dtype=np.float32)
          for k in ("conv_w", "conv_b", "dt_bias", "A_log", "D", "norm_w")}
    wr = {k: np.ascontiguousarray(np.asarray(inputs["L0_attn_%s_r" % k]),
                                  dtype=np.float32)
          for k in ("conv_w", "conv_b", "dt_bias", "A_log", "D", "norm_w")}
    b1 = np.ascontiguousarray(np.asarray(inputs["L0_ffn_b1"], dtype=np.float32))
    b2 = np.ascontiguousarray(np.asarray(inputs["L0_ffn_b2"], dtype=np.float32))

    in_maps = []
    for core in range(8):
        seq, half = core // 2, core % 2
        x_can = x[seq] if half == 0 else x[seq, ::-1]
        xT = np.ascontiguousarray(x_can.T)
        wA, wB = (wf, wr) if half == 0 else (wr, wf)
        m = {
            "xTf": xT,
            "x_tm": np.ascontiguousarray(x_can[:TOWN]),
            "WinT3": WinT3, "WoutT2": WoutT2,
            "U1b": U1b, "V1b": V1b, "U2b": U2b, "V2b": V2b,
            "b1": b1, "b2": b2,
            "fsel": np.full((128, 1), 1.0 - half, dtype=np.float32),
        }
        for tag, w in (("A", wA), ("B", wB)):
            m["cw" + tag] = w["conv_w"]
            m["cb" + tag] = w["conv_b"]
            m["dtb" + tag] = w["dt_bias"]
            m["Alog" + tag] = w["A_log"]
            m["D" + tag] = w["D"]
            m["nw" + tag] = w["norm_w"]
        in_maps.append(m)
    return in_maps


def kernel(**inputs):
    in_maps = prepare_inmaps(inputs)
    nc, _ = build_program()
    res = run_bass_kernel_spmd(nc, in_maps, core_ids=list(range(8)))
    out = np.zeros((4, 4096, 1024), dtype=np.float32)
    for core in range(8):
        seq, half = core // 2, core % 2
        o = np.asarray(res.results[core]["out"])
        if half == 0:
            out[seq, :TOWN] = o
        else:
            out[seq, TOWN:] = o[::-1]
    return out


# version 25 (final)
# speedup vs baseline: 1.0508x; 1.0508x over previous
import contextlib
import numpy as np

import concourse.bass as bass
import concourse.mybir as mybir
from concourse.bass_utils import run_bass_kernel_spmd
from concourse.alu_op_type import AluOpType as OP
from concourse import library_config

F32 = mybir.dt.float32
F16 = mybir.dt.float16
AF = mybir.ActivationFunctionType

DM, DIN, DST, NH, HD, Q = 1024, 2048, 128, 32, 64, 128
L, TOWN = 4096, 2048
NCK, OCK = L // Q, TOWN // Q
DIPP = 4480
NKK = DM // 128
NCC = DIPP // 128
RANK, DFF = 256, 4096
EPS = 1e-5


class P:
    def __init__(self, nc, block, stack):
        self.nc, self.block, self.stack = nc, block, stack
        self.ct, self.sems = {}, {}
        for name in ("qS", "qG", "sPE", "sDV", "sAC", "sGP"):
            self.sems[name] = stack.enter_context(nc.semaphore(name))
            self.ct[name] = 0

    def inc(self, instr, sem, n=1):
        self.ct[sem] += n
        instr.then_inc(self.sems[sem], n)
        return instr

    def wait(self, eng, *sems):
        for s in sems:
            if self.ct[s] > 0:
                eng.wait_ge(self.sems[s], self.ct[s])

    def mark(self, sem):
        return self.ct[sem]

    def wait_at(self, eng, sem, v):
        if v > 0:
            eng.wait_ge(self.sems[sem], v)

    def _ninstr(self):
        return sum(len(bb.instructions) for f in self.nc.m.functions
                   for bb in f.blocks)

    def add_sem(self, name):
        if name not in self.sems:
            self.sems[name] = self.stack.enter_context(self.nc.semaphore(name))
            self.ct[name] = 0
        return name

    def wait_all_dma(self, eng):
        for nm in list(self.sems):
            if nm.startswith("q"):
                self.wait(eng, nm)

    def wait_everything(self, eng):
        for nm in list(self.sems):
            self.wait(eng, nm)

    def dma(self, eng, sem, **kw):
        n0 = self._ninstr()
        ins = eng.dma_start(**kw)
        k = self._ninstr() - n0
        assert k >= 1
        ins.then_inc(self.sems[sem], 16)
        self.ct[sem] += 16 * k
        return ins

    def sb(self, name, shape, dtype=F32, stack=None):
        return (stack or self.stack).enter_context(
            self.nc.sbuf_tensor(name, shape, dtype))

    def ps(self, name, shape, dtype=F32, stack=None):
        return (stack or self.stack).enter_context(
            self.nc.psum_tensor(name, shape, dtype))


def _drive(gens):
    gens = list(gens)
    while gens:
        done = []
        for g in gens:
            try:
                next(g)
            except StopIteration:
                done.append(g)
        for g in done:
            gens.remove(g)


def hb(ap2, p=HD):
    return ap2.unsqueeze(2).broadcast_to([ap2.shape[0], ap2.shape[1], p])


def build_program(dump=(), stages=("s0", "s2z", "s2", "s4", "xc", "s5", "s5b", "s6")):
    nc = bass.Bass("TRN2", target_bir_lowering=False)
    io = {}

    def din(name, shape, dtype=F32):
        io[name] = nc.dram_tensor(name, shape, dtype, kind="ExternalInput").ap()

    def dscr(name, shape, dtype=F32):
        kind = "ExternalOutput" if name in dump else "Internal"
        io[name] = nc.dram_tensor(name, shape, dtype, kind=kind).ap()

    din("xTf", [DM, L])
    din("x_tm", [TOWN, DM])
    din("WinT3", [NCC, 128, NKK * 128])
    din("WoutT2", [2, 128, 8192])
    din("U1b", [NKK, 2, 128, 128])
    din("V1b", [2, 32, 128, 128])
    din("U2b", [32, 2, 128, 128])
    din("V2b", [2, 8, 128, 128])
    din("b1", [DFF])
    din("b2", [DM])
    din("fsel", [128, 1])
    for s in ("A", "B"):
        din("cw" + s, [2304, 4])
        din("cb" + s, [2304])
        din("dtb" + s, [NH])
        din("Alog" + s, [NH])
        din("D" + s, [NH])
        din("nw" + s, [DIN])
    io["out"] = nc.dram_tensor("out", [TOWN, DM], F32, kind="ExternalOutput").ap()

    dscr("ssz", [OCK, 128, DIN])
    dscr("sxsA", [OCK, 128, DIN])
    dscr("sxsB", [NCK, 128, DIN])
    dscr("sBA", [OCK, DST, Q])
    dscr("sBB", [NCK, DST, Q])
    dscr("sCA", [OCK, DST, Q])
    dscr("sCB", [OCK, DST, Q])
    for nm in ("sWzH", "sWzL"):
        io[nm] = nc.dram_tensor(nm, [4, 128, NKK * 512], F16,
                                kind="Internal").ap()
    for nm in ("sWfH", "sWfL"):
        io[nm] = nc.dram_tensor(nm, [19, 128, NKK * 128], F16,
                                kind="Internal").ap()
    dscr("sdtA", [NH, L])
    dscr("sdtB", [NH, L])
    dscr("srowD", [1, 2 * DIN])
    for tg in ("s4", "s5"):
        dscr("srowS" + tg, [1, 16 * NH])
        dscr("srowC" + tg, [1, 16 * NH * 128])
    io["cgin"] = nc.dram_tensor("cgin", [DST, DIN], F32, kind="Internal").ap()
    io["cgout"] = nc.dram_tensor("cgout", [2, DST, DIN], F32,
                                 kind="Internal").ap()
    dscr("sYA", [OCK, 128, DIN])
    dscr("sYS", [OCK, 128, DIN])
    dscr("sXLN", [TOWN, DM])

    with contextlib.ExitStack() as gs:
        block = gs.enter_context(nc.Block())
        p = P(nc, block, gs)
        C = {}
        C["ident"] = p.sb("ident", [128, 128])
        C["svec"] = p.sb("svec", [1, 128])
        C["b1r"] = p.sb("b1r", [128, 32])
        for nm in ("dtbA", "dtbB", "AA", "AB"):
            C[nm] = p.sb(nm + "_t", [32, 1])
        C["epsb"] = p.sb("epsb", [128, 1])
        C["cwA"] = p.sb("cwA_t", [128, 18 * 4])
        C["cwB"] = p.sb("cwB_t", [128, 18 * 4])
        C["cbA"] = p.sb("cbA_t", [128, 18])
        C["cbB"] = p.sb("cbB_t", [128, 18])

        with contextlib.ExitStack() as ms:
            C["maskA"] = p.sb("maskA", [128, 128], stack=ms)
            C["maskB"] = p.sb("maskB", [128, 128], stack=ms)
            C["nwAt"] = p.sb("nwAt", [128, DIN], stack=ms)
            C["nwBt"] = p.sb("nwBt", [128, DIN], stack=ms)
            C["dgDA"] = p.sb("dgDA", [128, DIN], stack=ms)
            C["dgDB"] = p.sb("dgDB", [128, DIN], stack=ms)
            srunA = p.sb("srunA", [DST, DIN], stack=ms)
            srunB = p.sb("srunB", [DST, DIN], stack=ms)

            emit_s0(p, io, C, ms)
            if "s2" in stages:
                with contextlib.ExitStack() as st:
                    emit_s1(p, io, C, st)
                with contextlib.ExitStack() as st:
                    emit_s2(p, io, C, st)
            gens, stacks = [], []
            if "s4" in stages:
                st4 = contextlib.ExitStack()
                gens.append(gen_ssd(p, io, C, st4, dirB=False,
                                    chunks=list(range(OCK)),
                                    compute_y=True, srun=srunA, cont=False,
                                    tag="s4", combine=False))
                stacks.append(st4)
            _drive(gens)
            for st_ in reversed(stacks):
                st_.close()
            if "xc" in stages:
                with contextlib.ExitStack() as stx:
                    emit_exchange(p, io, C, stx, srunA, srunB)
            if "s5" in stages:
                st5 = contextlib.ExitStack()
                st5b = contextlib.ExitStack()
                slog = []
                xlog = []
                T5 = {}
                g5 = gen_ssd(p, io, C, st5, dirB=True,
                             chunks=list(range(OCK - 1, -1, -1)),
                             compute_y=True, srun=srunB, cont=True,
                             tag="s5", combine=True, spill_log=slog,
                             tiles_out=T5)
                act = [g5]
                if "s5b" in stages:
                    act.append(gen_s5b(p, io, C, st5b, slog, xlog))
                g6 = gen_s6(p, io, C, T5, xlog) if "s6" in stages else None
                while act:
                    for g in list(act):
                        try:
                            next(g)
                        except StopIteration:
                            act.remove(g)
                            if g is g5 and g6 is not None:
                                act.append(g6)
                                g6 = None
                st5b.close()
                st5.close()

        @block.sync
        def _(sync):
            p.wait(sync, "qG", "qS")
    return nc, io


def emit_s0(p, io, C, ms):
    b = p.block
    s0s = contextlib.ExitStack()
    DtA = p.sb("DtA", [128, DIN], stack=s0s)
    DtB = p.sb("DtB", [128, DIN], stack=s0s)
    C["Drow"] = p.sb("Drow", [1, 2 * DIN], stack=s0s)
    ps0 = p.ps("s0_ps", [128, 512], stack=s0s)

    @b.sync
    def _(sync):
        for s in ("A", "B"):
            p.dma(sync, "qS", 
                out=C["cw" + s][:, :].rearrange("p (cc t) -> p cc t", cc=18),
                in_=io["cw" + s].rearrange("(cc p) t -> p cc t", p=128))
            p.dma(sync, "qS", 
                out=C["cb" + s][:, :],
                in_=io["cb" + s].rearrange("(cc p) -> p cc", p=128),
                allow_slow_non_contiguous=True)
            p.dma(sync, "qS", out=C["dtb" + s][:, 0:1],
                                 in_=io["dtb" + s][:].unsqueeze(1))
        p.dma(sync, "qS", out=C["svec"][0:1, 0:32],
                             in_=io["DA"][:].unsqueeze(0))
        p.dma(sync, "qS", out=C["svec"][0:1, 32:64],
                             in_=io["DB"][:].unsqueeze(0))
        p.dma(sync, "qS", out=C["svec"][0:1, 64:96],
                             in_=io["AlogA"][:].unsqueeze(0))
        p.dma(sync, "qS", out=C["svec"][0:1, 96:128],
                             in_=io["AlogB"][:].unsqueeze(0))

        p.dma(sync, "qS", out=C["b1r"][:, :],
                             in_=io["b1"].rearrange("(cc p) -> p cc", p=128),
                             allow_slow_non_contiguous=True)

    @b.vector
    def _(v):
        p.inc(v.memset(C["epsb"][:, :], EPS), "sDV")

    @b.scalar
    def _(sc):
        p.wait(sc, "qS")
        p.inc(sc.activation(C["svec"][0:1, 64:128], C["svec"][0:1, 64:128],
                            AF.Exp), "sAC")

    @b.vector
    def _(v):
        p.wait(v, "sAC")
        p.inc(v.tensor_scalar(C["svec"][0:1, 64:128], C["svec"][0:1, 64:128],
                              -1.0, None, OP.mult), "sDV")

    @b.gpsimd
    def _(g):
        p.wait(g, "qS")
        p.inc(g.memset(C["ident"][:, :], 0.0), "sGP")
        g.wait_ge(p.sems["sGP"], p.ct["sGP"])
        p.inc(g.affine_select(out=C["ident"][:, :], in_=C["ident"][:, :],
                              compare_op=OP.not_equal, fill=1.0, base=0,
                              pattern=[[-1, 128]], channel_multiplier=1), "sGP")
        g.wait_ge(p.sems["sGP"], p.ct["sGP"])
        p.inc(g.memset(C["maskA"][:, :], 0.0), "sGP")
        g.wait_ge(p.sems["sGP"], p.ct["sGP"])
        p.inc(g.affine_select(out=C["maskA"][:, :], in_=C["maskA"][:, :],
                              compare_op=OP.is_gt, fill=1.0, base=0,
                              pattern=[[-1, 128]], channel_multiplier=1), "sGP")
        g.wait_ge(p.sems["sGP"], p.ct["sGP"])
        p.inc(g.memset(C["maskB"][:, :], 1.0), "sGP")
        g.wait_ge(p.sems["sGP"], p.ct["sGP"])
        p.inc(g.affine_select(out=C["maskB"][:, :], in_=C["maskB"][:, :],
                              compare_op=OP.is_ge, fill=0.0, base=0,
                              pattern=[[-1, 128]], channel_multiplier=1), "sGP")

    @b.tensor
    def _(t):
        p.wait(t, "sDV", "sGP")
        t.transpose(ps0[0:32, 0:1], C["svec"][0:1, 64:96], C["ident"][0:1, 0:1])
        p.inc(t.transpose(ps0[0:32, 1:2], C["svec"][0:1, 96:128],
                          C["ident"][0:1, 0:1]), "sPE")

    @b.vector
    def _(v):
        p.wait(v, "sPE")
        v.tensor_copy(C["AA"][:, 0:1], ps0[0:32, 0:1])
        v.tensor_copy(C["AB"][:, 0:1], ps0[0:32, 1:2])
        v.tensor_copy(C["Drow"][0:1, 0:DIN].rearrange("o (h e) -> o h e", h=NH),
                      hb(C["svec"][0:1, 0:32]))
        p.inc(v.tensor_copy(
            C["Drow"][0:1, DIN:].rearrange("o (h e) -> o h e", h=NH),
            hb(C["svec"][0:1, 32:64])), "sDV")

    @b.sync
    def _(sync):
        p.wait(sync, "sDV")
        p.dma(sync, "qS", out=io["srowD"][:, :], in_=C["Drow"][:, :])
        p.wait(sync, "qS")
        p.dma(sync, "qS", 
            out=DtA[:, :],
            in_=io["srowD"][0:1, 0:DIN].partition_broadcast(128).squeeze(1))
        p.dma(sync, "qS", 
            out=DtB[:, :],
            in_=io["srowD"][0:1, DIN:].partition_broadcast(128).squeeze(1))
        p.dma(sync, "qS", 
            out=C["nwAt"][:, :],
            in_=io["nwA"][:].unsqueeze(0).partition_broadcast(128).squeeze(1))
        p.dma(sync, "qS", 
            out=C["nwBt"][:, :],
            in_=io["nwB"][:].unsqueeze(0).partition_broadcast(128).squeeze(1))

    @b.vector
    def _(v):
        p.wait(v, "sGP", "qS")
        ins = None
        for i in range(16):
            sl = slice(i * 128, (i + 1) * 128)
            v.tensor_tensor(C["dgDA"][:, sl], DtA[:, sl], C["ident"][:, :], OP.mult)
            ins = v.tensor_tensor(C["dgDB"][:, sl], DtB[:, sl], C["ident"][:, :],
                                  OP.mult)
        p.inc(ins, "sDV")
    s0s.close()


def emit_s1(p, io, C, st):
    b = p.block
    w32 = [p.sb("s1_w32a", [128, NKK * 128], stack=st),
           p.sb("s1_w32b", [128, NKK * 128], stack=st)]
    qs1 = [p.add_sem("qs1a"), p.add_sem("qs1b")]
    for i in range(4):
        p.add_sem("qs1z%d" % i)
    p.add_sem("qs1f")
    wh = p.sb("s1_wh", [128, NKK * 128], F16, stack=st)
    wl = p.sb("s1_wl", [128, NKK * 128], F16, stack=st)
    lm = [0, 0]
    spm = None
    for cc in range(NCC):
        par = cc % 2

        @b.sync
        def _(sync, cc=cc, par=par):
            p.wait_at(sync, "sDV", lm[par])
            if cc < 2:
                p.wait(sync, "qS", "sGP", "sDV")
            p.dma(sync, qs1[par], out=w32[par][:, :], in_=io["WinT3"][cc])
        qm = p.mark(qs1[par])

        @b.vector
        def _(v, cc=cc, par=par, qm=qm, spm=spm):
            p.wait_at(v, qs1[par], qm)
            if spm:
                p.wait(v, spm)
            p.inc(v.tensor_copy(wh[:, :], w32[par][:, :]), "sDV")
            p.wait(v, "sDV")
            p.inc(v.tensor_tensor(wl[:, :], w32[par][:, :], wh[:, :],
                                  OP.subtract), "sDV")
        lm[par] = p.mark("sDV")

        snm = ("qs1z%d" % (cc // 4)) if cc < 16 else "qs1f"

        @b.gpsimd
        def _(g, cc=cc, m=lm[par], snm=snm):
            p.wait_at(g, "sDV", m)
            if cc < 16:
                ct, l = cc // 4, cc % 4
                for wt, dst in ((wh, "sWzH"), (wl, "sWzL")):
                    p.dma(g, snm,
                          out=io[dst][ct].rearrange(
                              "p (kk l c) -> p kk l c", kk=NKK, l=4)[:, :, l, :],
                          in_=wt[:, :].rearrange("p (kk c) -> p kk c", kk=NKK))
            else:
                p.dma(g, snm, out=io["sWfH"][cc - 16], in_=wh[:, :])
                p.dma(g, snm, out=io["sWfL"][cc - 16], in_=wl[:, :])
        spm = snm


def emit_s2(p, io, C, st):
    b = p.block
    ring = [p.sb("f_ring0", [128, 18 * 520], stack=st),
            p.sb("f_ring1", [128, 18 * 520], stack=st)]
    xblk = p.sb("f_xblk", [128, NKK * 512], stack=st)
    xbh = p.sb("f_xbh", [128, NKK * 512], F16, stack=st)
    xbl = p.sb("f_xbl", [128, NKK * 512], F16, stack=st)
    wsh = [p.sb("f_wsh0", [128, NKK * 128], F16, stack=st),
           p.sb("f_wsh1", [128, NKK * 128], F16, stack=st)]
    wsl = [p.sb("f_wsl0", [128, NKK * 128], F16, stack=st),
           p.sb("f_wsl1", [128, NKK * 128], F16, stack=st)]
    wzh = p.sb("f_wzh", [128, NKK * 512], F16, stack=st)
    wzl = p.sb("f_wzl", [128, NKK * 512], F16, stack=st)
    zev2 = p.sb("f_zev2", [128, 2048], stack=st)
    cvo = [p.sb("f_cvo0", [128, 512], stack=st),
           p.sb("f_cvo1", [128, 512], stack=st)]
    cvtA = p.sb("f_cvtA", [128, 512], stack=st)
    cvtB = p.sb("f_cvtB", [128, 512], stack=st)
    dtacc = p.sb("f_dtacc", [32, TOWN], stack=st)
    pf = [p.ps("f_pf0", [128, 512], stack=st), p.ps("f_pf1", [128, 512], stack=st)]
    pz = p.ps("f_pz", [128, 2048], stack=st)

    def eap(buf, ccv):
        return buf[:, ccv * 520:ccv * 520 + 518]

    for nm in ("qX0", "qX1", "qW0", "qW1", "qZW", "qZE0", "qZE1",
               "qCV0", "qCV1", "qDT0", "qDT1", "qDP0", "qDP1", "qDS0", "qDS1"):
        p.add_sem(nm)
    dtaccm = [0]
    xmark = [0, 0]
    xsplit = [0]
    xuser = [0, 0]
    zwm = [0, 0]
    zwu = [0, 0]
    zpsa = [0, 0]
    zspill = [0, 0]
    wmark = [0, 0]
    fwu = [0, 0]
    pfev = [0, 0]
    sAm = [0] * 18
    sBm = [0] * 18
    gpm_ccv = [0] * 18
    cvt_free = [0, 0]
    cvsp = [0, 0]
    dtsp = [0, 0]
    cvj = 0

    zq = 0
    for tb in range(5):
        cur, prev = ring[tb % 2], ring[(tb + 1) % 2]
        own = tb < 4
        mini = tb == 4
        nt = 3 if mini else 512
        if True:
            par_x = tb % 2

            @b.sync
            def _(sync, tb=tb, nt=nt):
                p.wait_at(sync, "sDV", xsplit[0])
                if tb < 2:
                    p.wait(sync, "sDV")
                p.dma(sync, "qX0",
                    out=xblk[:, :].rearrange("p (kk t) -> p kk t",
                                             kk=NKK)[:, :, 0:nt],
                    in_=io["xTf"][:, tb * 512:tb * 512 + nt].rearrange(
                        "(kk p) t -> p kk t", p=128))
                xmark[0] = p.mark("qX0")

            @b.vector
            def _(v, tb=tb):
                p.wait_at(v, "qX0", xmark[0])
                p.wait_at(v, "sPE", xuser[0])
                p.inc(v.tensor_copy(xbh[:, :], xblk[:, :]), "sDV")
                p.wait(v, "sDV")
                p.inc(v.tensor_tensor(xbl[:, :], xblk[:, :], xbh[:, :],
                                      OP.subtract), "sDV")
            xsplit[0] = p.mark("sDV")

            if tb == 0:
                @b.vector
                def _(v):
                    p.wait_everything(v)
                    v.memset(ring[0][:, :], 0.0)
                    p.inc(v.memset(ring[1][:, :], 0.0), "sDV")

            if own:
                for ct in range(4):
                    wpar = ct % 2

                    @b.sync
                    def _(sync, ct=ct, wpar=wpar):
                        p.wait_at(sync, "sPE", max(zwu))
                        p.wait(sync, "qs1z%d" % ct)
                        p.dma(sync, "qZW", out=wzh[:, :], in_=io["sWzH"][ct])
                        p.dma(sync, "qZW", out=wzl[:, :], in_=io["sWzL"][ct])
                        zwm[wpar] = p.mark("qZW")

                    @b.tensor
                    def _(t, ct=ct, wpar=wpar):
                        p.wait_at(t, "qZW", zwm[wpar])
                        p.wait_at(t, "sDV", xsplit[0])
                        p.wait_at(t, "sAC", zpsa[0])
                        ins = None
                        for tc in range(4):
                            for kk in range(NKK):
                                sl = slice(kk * 512 + tc * 128,
                                           kk * 512 + (tc + 1) * 128)
                                ws_ = slice(kk * 512, (kk + 1) * 512)
                                psl = slice(tc * 512, (tc + 1) * 512)
                                t.matmul(pz[:, psl], xbh[:, sl], wzh[:, ws_],
                                         start=(kk == 0), stop=False)
                                t.matmul(pz[:, psl], xbh[:, sl], wzl[:, ws_],
                                         start=False, stop=False)
                                ins = t.matmul(pz[:, psl], xbl[:, sl],
                                               wzh[:, ws_], start=False,
                                               stop=(kk == NKK - 1))
                        p.inc(ins, "sPE")
                    zmmark = p.mark("sPE")
                    zwu[wpar] = zmmark
                    xuser[0] = zmmark

                    @b.scalar
                    def _(sc, zmmark=zmmark):
                        p.wait_at(sc, "sPE", zmmark)
                        p.wait_at(sc, "qZE0", zspill[0])
                        p.inc(sc.activation(zev2[:, :], pz[:, :], AF.Silu),
                              "sAC")
                    zpsa[0] = p.mark("sAC")

                    @b.gpsimd
                    def _(g, tb=tb, ct=ct):
                        p.wait_at(g, "sAC", zpsa[0])
                        p.dma(g, "qZE0",
                              out=io["ssz"][tb * 4:(tb + 1) * 4, :,
                                            ct * 512:(ct + 1) * 512].rearrange(
                                  "j p c -> p j c"),
                              in_=zev2[:, :].rearrange("p (j c) -> p j c", j=4))
                    zspill[0] = p.mark("qZE0")

            for cc in range(16, 34 if mini else 35):
                ccv = cc - 16
                wpar = cc % 2

                @b.sync
                def _(sync, cc=cc, wpar=wpar):
                    p.wait_at(sync, "sPE", fwu[wpar])
                    p.wait(sync, "qs1f")
                    p.dma(sync, "qW%d" % wpar,
                          out=wsh[wpar][:, :], in_=io["sWfH"][cc - 16])
                    p.dma(sync, "qW%d" % wpar,
                          out=wsl[wpar][:, :], in_=io["sWfL"][cc - 16])
                    wmark[wpar] = p.mark("qW%d" % wpar)

                ppar = ccv % 2

                @b.tensor
                def _(t, cc=cc, wpar=wpar, ppar=ppar, nt=nt):
                    p.wait_at(t, "qW%d" % wpar, wmark[wpar])
                    p.wait_at(t, "sDV", xsplit[0])
                    p.wait_at(t, "sDV", pfev[ppar])
                    ins = None
                    for kk in range(NKK):
                        ws_ = slice(kk * 128, (kk + 1) * 128)
                        xs_ = slice(kk * 512, kk * 512 + nt)
                        t.matmul(pf[ppar][:, 0:nt], wsh[wpar][:, ws_],
                                 xbh[:, xs_], start=(kk == 0), stop=False)
                        t.matmul(pf[ppar][:, 0:nt], wsh[wpar][:, ws_],
                                 xbl[:, xs_], start=False, stop=False)
                        ins = t.matmul(pf[ppar][:, 0:nt], wsl[wpar][:, ws_],
                                       xbh[:, xs_], start=False,
                                       stop=(kk == NKK - 1))
                    p.inc(ins, "sPE")
                fmm = p.mark("sPE")
                fwu[wpar] = fmm
                xuser[0] = fmm

                @b.vector
                def _(v, cc=cc, ccv=ccv, ppar=ppar, fmm=fmm, cur=cur, tb=tb,
                      nt=nt):
                    p.wait_at(v, "sPE", fmm)
                    p.wait(v, "sDV")
                    if cc == 34:
                        p.inc(v.tensor_copy(
                            dtacc[:, tb * 512:(tb + 1) * 512],
                            pf[ppar][0:32, :]), "sDV")
                        dtaccm[0] = p.mark("sDV")
                    else:
                        p.inc(v.tensor_copy(eap(cur, ccv)[:, 3:3 + nt],
                                            pf[ppar][:, 0:nt]), "sDV")
                pfev[ppar] = p.mark("sDV")


        if tb >= 1:
            ptb = tb - 1
            pown = ptb < 4
            for ccv in range(18):
                have_next = True

                @b.vector
                def _(v, ccv=ccv, cur=cur, prev=prev, have_next=have_next):
                    p.wait(v, "sDV")
                    if have_next:
                        v.tensor_copy(eap(prev, ccv)[:, 515:518],
                                      eap(cur, ccv)[:, 3:6])
                        p.wait(v, "sDV")
                        p.inc(v.tensor_copy(eap(cur, ccv)[:, 0:3],
                                            eap(prev, ccv)[:, 512:515]), "sDV")
                    else:
                        p.inc(v.memset(eap(prev, ccv)[:, 515:518], 0.0), "sDV")

                convs = []
                if pown:
                    convs.append(False)
                if ccv <= 16 or pown:
                    convs.append(True)
                for dirB in convs:
                    w = C["cwB" if dirB else "cwA"]
                    cb = C["cbB" if dirB else "cbA"]
                    offs = (6, 5, 4, 3) if dirB else (0, 1, 2, 3)
                    cvt = cvtB if dirB else cvtA
                    smarks = sBm if dirB else sAm
                    par = cvj % 2
                    cvj += 1

                    @b.vector
                    def _(v, prev=prev, offs=offs, w=w, ccv=ccv, cvt=cvt,
                          dirB=dirB):
                        p.wait(v, "sDV")
                        p.wait_at(v, "sAC", cvt_free[int(dirB)])
                        p.inc(v.tensor_scalar(
                            cvt[:, :],
                            eap(prev, ccv)[:, offs[0]:offs[0] + 512],
                            w[:, ccv * 4:ccv * 4 + 1], None, OP.mult),
                            "sDV")
                        for k in (1, 2, 3):
                            p.wait(v, "sDV")
                            p.inc(v.scalar_tensor_tensor(
                                cvt[:, :],
                                eap(prev, ccv)[:, offs[k]:offs[k] + 512],
                                w[:, ccv * 4 + k:ccv * 4 + k + 1],
                                cvt[:, :], OP.mult, OP.add), "sDV")
                    cvm = p.mark("sDV")

                    @b.scalar
                    def _(sc, cb=cb, ccv=ccv, par=par, cvm=cvm, cvt=cvt,
                          dirB=dirB):
                        p.wait_at(sc, "sDV", cvm)
                        p.wait_at(sc, "qCV%d" % par, cvsp[par])
                        p.inc(sc.activation(cvo[par][:, :], cvt[:, :], AF.Silu,
                                            bias=cb[:, ccv:ccv + 1]), "sAC")
                    smarks[ccv] = p.mark("sAC")
                    cvt_free[int(dirB)] = smarks[ccv]

                    if ccv < 16:
                        dname = "sxsB" if dirB else "sxsA"
                        row0 = ccv * 128
                    elif ccv == 16:
                        dname = "sBB" if dirB else "sBA"
                        row0 = 0
                    else:
                        dname = "sCB" if dirB else "sCA"
                        row0 = 0

                    @b.gpsimd
                    def _(g, dname=dname, row0=row0, ptb=ptb, par=par,
                          sm=smarks[ccv], ccv=ccv):
                        p.wait_at(g, "sAC", sm)
                        if ccv < 16:
                            dst = io[dname][ptb * 4:(ptb + 1) * 4, :,
                                            ccv * 128:(ccv + 1) * 128]
                        else:
                            dst = io[dname][ptb * 4:(ptb + 1) * 4,
                                            row0:row0 + 128, :]
                        p.dma(g, "qCV%d" % par,
                            out=dst.rearrange("j p t -> p j t"),
                            in_=cvo[par][:, :].rearrange("p (j t) -> p j t",
                                                         j=4))
                    cvsp[par] = p.mark("qCV%d" % par)

    dte = ring[0][0:32, 0:TOWN]
    for dd, dtb, sname in (("A", "dtbA", "sdtA"), ("B", "dtbB", "sdtB")):
        @b.scalar
        def _(sc, dtb=dtb):
            p.wait(sc, "sDV")
            p.wait(sc, "qDT0")
            p.inc(sc.activation(dte[:, :], dtacc[:, :], AF.Exp,
                                bias=C[dtb][:, 0:1]), "sAC")
            sc.wait_ge(p.sems["sAC"], p.ct["sAC"])
            p.inc(sc.activation(dte[:, :], dte[:, :], AF.Ln, bias=1.0),
                  "sAC")

        @b.gpsimd
        def _(g, sname=sname):
            p.wait(g, "sAC")
            p.dma(g, "qDT0", out=io[sname][:, 0:TOWN], in_=dte[:, :])


def emit_exchange(p, io, C, st, srunA, srunB):
    b = p.block
    QX = p.add_sem("qxcL")
    g0 = p.sb("xc_g0", [DST, DIN], stack=st)
    g1 = p.sb("xc_g1", [DST, DIN], stack=st)
    fselt = p.sb("xc_fsel", [DST, 1], stack=st)

    @b.sync
    def _(sync):
        p.wait_everything(sync)
        p.dma(sync, QX, out=io["cgin"][:, :], in_=srunA[:, :])
        p.dma(sync, QX, out=fselt[:, :], in_=io["fsel"][:, :])

    @b.gpsimd
    def _(g):
        p.wait(g, QX)
        p.inc(g.collective_compute(
            "AllGather", mybir.AluOpType.bypass,
            ins=[io["cgin"][:, :]], outs=[io["cgout"][:, :, :]],
            replica_groups=[[0, 1], [2, 3], [4, 5], [6, 7]]), "sGP")

    @b.sync
    def _(sync):
        p.wait(sync, "sGP")
        p.dma(sync, QX, out=g0[:, :], in_=io["cgout"][0])
        p.dma(sync, QX, out=g1[:, :], in_=io["cgout"][1])

    @b.vector
    def _(v):
        p.wait(v, QX)
        p.inc(v.tensor_tensor(g1[:, :], g1[:, :], g0[:, :], OP.subtract),
              "sDV")
        p.wait(v, "sDV")
        p.inc(v.scalar_tensor_tensor(srunB[:, :], g1[:, :], fselt[:, 0:1],
                                     g0[:, :], OP.mult, OP.add), "sDV")


def gen_ssd(p, io, C, st, dirB, chunks, compute_y, srun, cont, tag, combine,
            spill_log=None, tiles_out=None):
    b = p.block
    t = tag
    A_t = C["AB" if dirB else "AA"]
    mask = C["maskB" if dirB else "maskA"]
    dgD = C["dgDB" if dirB else "dgDA"]
    nwt = C["nwBt" if dirB else "nwAt"]
    sxs, sB, sC = ("sxsB", "sBB", "sCB") if dirB else ("sxsA", "sBA", "sCA")
    sdt = "sdtB" if dirB else "sdtA"
    SP, SD, SA, SG = (p.add_sem(t + x) for x in ("PE", "DV", "AC", "GP"))
    QL = p.add_sem("q" + t + "L")
    QG = p.add_sem("q" + t + "G")
    QG1 = p.add_sem("q" + t + "G1")
    QE = p.add_sem("q" + t + "E")
    QC = p.add_sem("q" + t + "C")
    QF = [p.add_sem("q" + t + "F%d" % i) for i in range(3)]
    QK = [p.add_sem("q" + t + "K%d" % i) for i in range(2)]
    QZ = p.add_sem("q" + t + "Z")

    cbase = min(chunks)
    tsel = 0 if dirB else 127

    crow = p.sb(t + "_crow", [128, NH * 128], stack=st)
    dtall = crow[0:32, 0:16 * Q]
    acall = dtall
    stmp = p.sb(t + "_stmp", [128, 1024], stack=st)
    ttm = p.sb(t + "_ttm", [128, 1024], stack=st)
    actm = p.sb(t + "_actm", [128, 512], stack=st)
    asall = p.sb(t + "_asall", [128, 512], stack=st)
    easall = asall
    ddtm = p.sb(t + "_ddtm", [128, 512], stack=st)
    if compute_y:
        eactm = p.sb(t + "_eactm", [128, 512], stack=st)
    assert compute_y
    xsckP = [p.sb(t + "_xsck%d" % i, [128, 16 * 128], stack=st)
             for i in range(3)]
    bckP = [p.sb(t + "_bck%d" % i, [128, 128], stack=st) for i in range(3)]
    xdP = [p.sb(t + "_xd%d" % i, [128, DIN], stack=st) for i in range(2)]
    xddP = [p.sb(t + "_xdd%d" % i, [128, DIN], stack=st) for i in range(2)]
    btmP = [p.sb(t + "_btm%d" % i, [128, 128], stack=st) for i in range(2)]
    cckP = [p.sb(t + "_cck%d" % i, [128, 128], stack=st) for i in range(2)]
    mallP = [p.sb(t + "_mall%d" % i, [128, NH * 128], stack=st)
             for i in range(2)]
    gm = p.sb(t + "_gm", [128, 128], stack=st)
    ysb = p.sb(t + "_ysb", [128, DIN], stack=st)
    szck = p.sb(t + "_szck", [128, DIN], stack=st)
    ssq = p.sb(t + "_ssq", [128, 4], stack=st)
    pX = p.ps(t + "_pX", [128, 1024], stack=st)
    pH1 = p.ps(t + "_pH1", [128, 1024], stack=st)
    pH2 = p.ps(t + "_pH2", [128, 1024], stack=st)
    if tiles_out is not None:
        tiles_out.update(mallP=mallP, crow=crow, xsckP=xsckP, xdP=xdP,
                         xddP=xddP, ysb=ysb, szck=szck, stmp=stmp, ttm=ttm,
                         actm=actm, asall=asall, ddtm=ddtm, eactm=eactm,
                         bckP=bckP, pX=pX, pH1=pH1)

    @b.vector
    def _(v):
        p.wait(v, "sPE", "sDV", "sAC", "sGP")
        p.wait(v, SP, SD, SA)
        p.wait_all_dma(v)
        if not cont:
            p.inc(v.memset(srun[:, :], 0.0), SD)
        else:
            p.inc(v.tensor_copy(srun[0:1, 0:1], srun[0:1, 0:1]), SD)

    pfm = [0, 0, 0]
    Ym, Sm = {}, {}
    cckm, crm, msubm, gmm, gmevm = {}, {}, {}, {}, {}
    mkXe, gatem, nwm, ysbevm, szm, tailv = {}, {}, {}, {}, {}, {}
    srunm, sqm = {}, {}

    def prefetch(sync, j):
        p3 = j % 3
        c = chunks[j]
        p.wait_at(sync, SP, Ym.get(j - 3, 0))
        p.dma(sync, QF[p3], out=xsckP[p3][:, :], in_=io[sxs][c])
        p.dma(sync, QF[p3], out=bckP[p3][:, :], in_=io[sB][c])
        pfm[p3] = p.mark(QF[p3])

    @b.sync
    def _(sync):
        p.wait_everything(sync)
        p.dma(sync, QL, out=dtall[:, :],
              in_=io[sdt][:, cbase * Q:(cbase + 16) * Q])
        prefetch(sync, 0)
    yield

    @b.tensor
    def _(t_):
        p.wait(t_, QL, SD)
        ins = None
        for k in range(16):
            ins = t_.transpose(pX[:, k * 32:(k + 1) * 32],
                               dtall[:, k * Q:(k + 1) * Q],
                               C["ident"][0:32, 0:32])
        p.inc(ins, SP)

    @b.vector
    def _(v):
        p.wait(v, SP)
        v.tensor_copy(ttm[:, 0:512], pX[:, 0:512])
        p.inc(v.tensor_scalar(dtall[:, :], dtall[:, :], A_t[:, 0:1], None,
                              OP.mult), SD)
    yield

    @b.tensor
    def _(t_):
        p.wait(t_, SD)
        ins = None
        for k in range(16):
            ins = t_.transpose(pX[:, 512 + k * 32:512 + (k + 1) * 32],
                               dtall[:, k * Q:(k + 1) * Q],
                               C["ident"][0:32, 0:32])
        p.inc(ins, SP)

    @b.vector
    def _(v):
        p.wait(v, SP)
        p.inc(v.tensor_copy(ttm[:, 512:1024], pX[:, 512:1024]), SD)

    @b.tensor
    def _(t_):
        p.wait(t_, SD)
        ins = None
        for k in range(16):
            ins = t_.matmul(pX[:, k * 32:(k + 1) * 32], mask[:, :],
                            ttm[:, 512 + k * 32:512 + (k + 1) * 32],
                            start=True, stop=True)
        p.inc(ins, SP)
    yield

    @b.vector
    def _(v):
        p.wait(v, SP)
        p.inc(v.tensor_copy(actm[:, :], pX[:, 0:512]), SD)

    if compute_y:
        @b.scalar
        def _(sc):
            p.wait(sc, SD)
            p.inc(sc.activation(eactm[:, :], actm[:, :], AF.Exp), SA)

        for half in range(2):
            @b.tensor
            def _(t_, half=half):
                p.wait(t_, SD)
                ins = None
                for k in range(8):
                    kk = half * 8 + k
                    ins = t_.matmul(pX[0:32, k * 128:(k + 1) * 128],
                                    ttm[:, 512 + kk * 32:512 + (kk + 1) * 32],
                                    mask[:, :], start=True, stop=True)
                p.inc(ins, SP)

            @b.vector
            def _(v, half=half):
                p.wait(v, SP)
                p.inc(v.tensor_copy(acall[:, half * 1024:(half + 1) * 1024],
                                    pX[0:32, 0:1024]), SD)
        yield

    @b.sync
    def _(sync):
        p.wait(sync, SD)
        if compute_y:
            p.wait(sync, SA)
            for k in range(16):
                p.dma(sync, QL,
                      out=io["srowC" + t][0:1, k * 4096:(k + 1) * 4096]
                      .rearrange("o (h tt) -> o h tt", h=NH),
                      in_=acall[:, k * 128:(k + 1) * 128].unsqueeze(1))
        p.dma(sync, QL, out=io["srowS" + t][:, :],
              in_=actm[tsel:tsel + 1, :])
        p.wait(sync, QL)
        p.dma(sync, QE, out=asall[:, :],
              in_=io["srowS" + t][0:1, :].partition_broadcast(128).squeeze(1))

    @b.vector
    def _(v):
        p.wait(v, QE)
        p.inc(v.tensor_tensor(ddtm[:, :], asall[:, :], actm[:, :],
                              OP.subtract), SD)

    @b.scalar
    def _(sc):
        p.wait(sc, SD)
        sc.activation(easall[:, :], asall[:, :], AF.Exp)
        p.inc(sc.activation(ddtm[:, :], ddtm[:, :], AF.Exp), SA)

    @b.vector
    def _(v):
        p.wait(v, SA)
        p.inc(v.tensor_tensor(ddtm[:, :], ddtm[:, :], ttm[:, 0:512],
                              OP.mult), SD)
    yield

    prep_sd = p.mark(SD)
    prep_sa = p.mark(SA)
    n = len(chunks)

    def emit_prep(ci):
        c = chunks[ci]
        kk2 = c - cbase
        p2 = ci % 2
        p3 = ci % 3

        @b.sync
        def _(sync, ci=ci, c=c, kk2=kk2, p2=p2):
            p.wait_at(sync, SP, Ym.get(ci - 2, 0))
            p.dma(sync, QK[p2], out=cckP[p2][:, :], in_=io[sC][c])
            cckm[ci] = p.mark(QK[p2])
            p.wait_at(sync, SD, msubm.get(ci - 1, 0))
            p.dma(sync, QC, out=crow[:, :],
                  in_=io["srowC" + t][0:1, kk2 * 4096:(kk2 + 1) * 4096]
                  .partition_broadcast(128).squeeze(1))
            crm[ci] = p.mark(QC)
            if ci + 1 < n:
                prefetch(sync, ci + 1)

        xevm = gmevm.get(ci - 1, prep_sd)
        for hf in range(2):
            @b.tensor
            def _(t_, ci=ci, hf=hf, p3=p3, xevm=xevm):
                p.wait_at(t_, QF[p3], pfm[p3])
                p.wait_at(t_, SD, xevm)
                ins = None
                for i in range(8):
                    ins = t_.transpose(pX[:, i * 128:(i + 1) * 128],
                                       xsckP[p3][:, (hf * 8 + i) * 128:
                                                 (hf * 8 + i + 1) * 128],
                                       C["ident"][:, :])
                p.inc(ins, SP)
            xtm = p.mark(SP)

            @b.vector
            def _(v, hf=hf, ci=ci, kk2=kk2, p2=p2, xtm=xtm):
                p.wait_at(v, SP, xtm)
                p.wait_at(v, SP, Sm.get(ci - 2, 0))
                p.wait_at(v, SA, sqm.get(ci - 2, 0))
                sl = slice(hf * 1024, (hf + 1) * 1024)
                v.tensor_tensor(
                    xdP[p2][:, sl].rearrange("p (h e) -> p h e", h=16),
                    pX[:, :].rearrange("p (h e) -> p h e", h=16),
                    hb(ttm[:, kk2 * 32 + 16 * hf:kk2 * 32 + 16 * (hf + 1)]),
                    OP.mult)
                p.inc(v.tensor_tensor(
                    xddP[p2][:, sl].rearrange("p (h e) -> p h e", h=16),
                    pX[:, :].rearrange("p (h e) -> p h e", h=16),
                    hb(ddtm[:, kk2 * 32 + 16 * hf:kk2 * 32 + 16 * (hf + 1)]),
                    OP.mult), SD)
            xevm = p.mark(SD)
        mkXe[ci] = p.mark(SD)

        @b.tensor
        def _(t_, ci=ci, p2=p2, p3=p3):
            p.wait_at(t_, SD, mkXe[ci])
            t_.transpose(pX[:, 0:128], bckP[p3][:, :], C["ident"][:, :])
            p.wait_at(t_, QK[p2], cckm[ci])
            p.inc(t_.matmul(pX[:, 128:256], bckP[p3][:, :], cckP[p2][:, :],
                            start=True, stop=True), SP)
        bgm = p.mark(SP)

        @b.vector
        def _(v, ci=ci, p2=p2, bgm=bgm):
            p.wait_at(v, SP, bgm)
            p.wait_at(v, SP, Sm.get(ci - 2, 0))
            v.tensor_copy(btmP[p2][:, :], pX[:, 0:128])
            p.inc(v.tensor_tensor(gm[:, :], pX[:, 128:256], mask[:, :],
                                  OP.mult), SD)
        gmevm[ci] = p.mark(SD)

        @b.vector
        def _(v, ci=ci, kk2=kk2, p2=p2):
            p.wait_at(v, QC, crm[ci])
            p.wait_at(v, SP, Ym.get(ci - 2, 0))
            p.inc(v.tensor_tensor(
                mallP[p2][:, :].rearrange("p (h l) -> p h l", h=NH),
                crow[:, :].rearrange("p (h l) -> p h l", h=NH),
                actm[:, kk2 * 32:(kk2 + 1) * 32].unsqueeze(2)
                .broadcast_to([128, NH, 128]),
                OP.subtract), SD)
            p.wait(v, SD)
            p.inc(v.tensor_scalar(mallP[p2][:, :], mallP[p2][:, :], 30.0,
                                  None, OP.min), SD)
        msubm[ci] = p.mark(SD)

        @b.scalar
        def _(sc, ci=ci, p2=p2):
            p.wait_at(sc, SD, msubm[ci])
            p.inc(sc.activation(mallP[p2][:, :], mallP[p2][:, :], AF.Exp),
                  SA)
        mexm = p.mark(SA)

        @b.vector
        def _(v, p2=p2, mexm=mexm):
            p.wait_at(v, SA, mexm)
            p.inc(v.tensor_tensor(
                mallP[p2][:, :].rearrange("p (h l) -> p h l", h=NH),
                mallP[p2][:, :].rearrange("p (h l) -> p h l", h=NH),
                gm[:, :].unsqueeze(1).broadcast_to([128, NH, 128]),
                OP.mult), SD)
        gmm[ci] = p.mark(SD)

    def emit_tail(ci):
        c = chunks[ci]
        kk2 = c - cbase
        p2 = ci % 2
        p3 = ci % 3

        @b.sync
        def _(sync, ci=ci, c=c):
            p.wait_at(sync, SD, gatem.get(ci - 1, 0))
            p.dma(sync, QZ, out=szck[:, :], in_=io["ssz"][c])
        szm[ci] = p.mark(QZ)

        evm0 = 0
        for hf in range(2):
            @b.tensor
            def _(t_, ci=ci, hf=hf, p2=p2, p3=p3, evm0=evm0):
                p.wait_at(t_, SD, max(gmm[ci], tailv.get(ci - 1, 0), evm0))
                for j in range(2):
                    t_.matmul(pH1[:, j * 512:(j + 1) * 512], cckP[p2][:, :],
                              srun[:, hf * 1024 + j * 512:
                                   hf * 1024 + (j + 1) * 512],
                              start=True, stop=True)
                for i in range(8):
                    t_.matmul(pH2[:, i * 128:(i + 1) * 128],
                              xsckP[p3][:, (hf * 8 + i) * 128:
                                        (hf * 8 + i + 1) * 128],
                              dgD[:, (hf * 8 + i) * 128:
                                  (hf * 8 + i + 1) * 128],
                              start=(i % 4 == 0), stop=False)
                ins = None
                for h in range(16):
                    hh = hf * 16 + h
                    ins = t_.matmul(pH2[:, h * HD:(h + 1) * HD],
                                    mallP[p2][:, hh * 128:(hh + 1) * 128],
                                    xdP[p2][:, hh * HD:(hh + 1) * HD],
                                    start=False, stop=(h % 8 == 7))
                p.inc(ins, SP)
            ym = p.mark(SP)

            @b.vector
            def _(v, hf=hf, ci=ci, kk2=kk2, ym=ym):
                p.wait_at(v, SP, ym)
                if hf == 0:
                    p.wait(v, QG, QG1)
                sl = slice(hf * 1024, (hf + 1) * 1024)
                p.inc(v.tensor_tensor(
                    ysb[:, sl].rearrange("p (h e) -> p h e", h=16),
                    pH1[:, :].rearrange("p (h e) -> p h e", h=16),
                    hb(eactm[:, kk2 * 32 + 16 * hf:
                             kk2 * 32 + 16 * (hf + 1)]), OP.mult), SD)
                p.wait(v, SD)
                p.inc(v.tensor_tensor(ysb[:, sl], ysb[:, sl], pH2[:, :],
                                      OP.add), SD)
            evm0 = p.mark(SD)
        Ym[ci] = p.mark(SP)
        ysbevm[ci] = p.mark(SD)

        @b.vector
        def _(v, ci=ci, p2=p2):
            p.wait_at(v, QZ, szm[ci])
            p.wait(v, SD)
            p.inc(v.tensor_tensor(ysb[:, :], ysb[:, :], szck[:, :],
                                  OP.mult), SD)
        gatem[ci] = p.mark(SD)

        @b.scalar
        def _(sc, ci=ci, p2=p2):
            p.wait_at(sc, SD, gatem[ci])
            p.inc(sc.activation(xdP[p2][:, :], ysb[:, :], AF.Square,
                                accum_out=ssq[:, 0:1]), SA)
            sc.wait_ge(p.sems[SA], p.ct[SA])
            p.inc(sc.activation(ssq[:, 1:2], ssq[:, 0:1], AF.Ln,
                                scale=1.0 / DIN, bias=C["epsb"][:, 0:1]),
                  SA)
            sc.wait_ge(p.sems[SA], p.ct[SA])
            p.inc(sc.activation(ssq[:, 2:3], ssq[:, 1:2], AF.Exp,
                                scale=-0.5), SA)
        rmsm = p.mark(SA)
        sqm[ci] = rmsm

        @b.vector
        def _(v, rmsm=rmsm):
            p.wait_at(v, SA, rmsm)
            p.inc(v.scalar_tensor_tensor(ysb[:, :], ysb[:, :], ssq[:, 2:3],
                                         nwt[:, :], OP.mult, OP.mult), SD)
        nwm[ci] = p.mark(SD)

        srh = ysbevm[ci]
        addm = srunm.get(ci - 1, 0)
        for hf in range(2):
            @b.tensor
            def _(t_, ci=ci, hf=hf, p2=p2, srh=srh):
                p.wait_at(t_, SD, srh)
                ins = None
                for j in range(2):
                    ins = t_.matmul(pH1[:, j * 512:(j + 1) * 512],
                                    btmP[p2][:, :],
                                    xddP[p2][:, hf * 1024 + j * 512:
                                             hf * 1024 + (j + 1) * 512],
                                    start=True, stop=True)
                p.inc(ins, SP)
            stm = p.mark(SP)

            @b.gpsimd
            def _(g, hf=hf, kk2=kk2, addm=addm):
                p.wait_at(g, SD, addm)
                p.wait_at(g, SA, prep_sa)
                p.wait(g, QE)
                sl = slice(hf * 1024, (hf + 1) * 1024)
                p.inc(g.tensor_tensor(
                    stmp[:, :].rearrange("p (h e) -> p h e", h=16),
                    srun[:, sl].rearrange("p (h e) -> p h e", h=16),
                    hb(easall[:, kk2 * 32 + 16 * hf:kk2 * 32 + 16 * (hf + 1)]),
                    OP.mult), SG)
            multm = p.mark(SG)

            @b.vector
            def _(v, hf=hf, stm=stm, multm=multm):
                p.wait_at(v, SP, stm)
                p.wait_at(v, SG, multm)
                sl = slice(hf * 1024, (hf + 1) * 1024)
                p.inc(v.tensor_tensor(srun[:, sl], pH1[:, :], stmp[:, :],
                                      OP.add), SD)
            addm = p.mark(SD)
            srh = addm
        srunm[ci] = addm
        Sm[ci] = p.mark(SP)
        tailv[ci] = p.mark(SD)

        qg = (QG, QG1)[len(spill_log) % 2] if spill_log is not None else QG

        @b.gpsimd
        def _(g, c=c, ci=ci, combine=combine, qg=qg):
            p.wait_at(g, SD, nwm[ci])
            if combine:
                p.dma(g, qg, out=io["sYS"][c], in_=ysb[:, :],
                      accum_op=OP.add)
            else:
                p.dma(g, qg, out=io["sYS"][c], in_=ysb[:, :])
        if spill_log is not None:
            spill_log.append((qg, p.mark(qg)))

    emit_prep(0)
    yield
    emit_prep(1)
    yield
    for i in range(n):
        emit_tail(i)
        yield "c"
        if i + 2 < n:
            emit_prep(i + 2)
            yield


def gen_s5b(p, io, C, st, spill_log, xlog):
    b = p.block
    SP, SD, SA = (p.add_sem("sb" + x) for x in ("PE", "DV", "AC"))
    QL = p.add_sem("qsbL")
    QWP = [p.add_sem("qsbW%d" % i) for i in range(2)]
    QG = p.add_sem("qsbG")
    QG1 = p.add_sem("qsbG1")
    woutP = [p.sb("o_wout%d" % i, [128, 2 * 512], stack=st)
             for i in range(2)]
    ysum = p.sb("o_ysum", [128, DIN], stack=st)
    yTf = ysum
    xres = ysum[:, 0:DM]
    dmp = ysum[:, DM:2 * DM]
    xtm = ysum[:, DM:2 * DM]
    sm = p.sb("o_sm", [128, 8], stack=st)
    pO = p.ps("o_pO", [128, 1024], stack=st)

    @b.sync
    def _(sync):
        p.wait(sync, "sPE", "sDV", "sAC", "sGP")
        p.wait_all_dma(sync)
        sync.nop()
    yield

    for i, c in enumerate(range(OCK - 1, -1, -1)):
        while len(spill_log) <= i:
            yield

        @b.sync
        def _(sync, c=c, i=i):
            p.wait(sync, SD, QG, QG1)
            sem, mk = spill_log[i]
            p.wait_at(sync, sem, mk)
            p.dma(sync, QL, out=ysum[:, :], in_=io["sYS"][c])
        yield

        for hf in range(2):
            @b.tensor
            def _(t_, hf=hf):
                p.wait(t_, SD, QL)
                ins = None
                for i in range(8):
                    ins = t_.transpose(pO[:, i * 128:(i + 1) * 128],
                                       ysum[:, (hf * 8 + i) * 128:
                                            (hf * 8 + i + 1) * 128],
                                       C["ident"][:, :])
                p.inc(ins, SP)

            @b.vector
            def _(v, hf=hf):
                p.wait(v, SP)
                p.inc(v.tensor_copy(yTf[:, hf * 1024:(hf + 1) * 1024],
                                    pO[:, :]), SD)
        yield

        wpm = [0, 0]
        for q in range(16):
            dmt, kq = q // 8, q % 8
            wp = q % 2

            @b.sync
            def _(sync, dmt=dmt, kq=kq, wp=wp):
                p.wait_at(sync, SP, wpm[wp])
                p.dma(sync, QWP[wp], out=woutP[wp][:, :],
                      in_=io["WoutT2"][dmt][:, kq * 1024:(kq + 1) * 1024])
            wm = p.mark(QWP[wp])

            @b.tensor
            def _(t_, dmt=dmt, kq=kq, wp=wp, wm=wm):
                p.wait_at(t_, QWP[wp], wm)
                p.wait(t_, SD)
                ins = None
                for k in range(2):
                    kk = kq * 2 + k
                    ins = t_.matmul(pO[:, dmt * 512:(dmt + 1) * 512],
                                    yTf[:, kk * 128:(kk + 1) * 128],
                                    woutP[wp][:, k * 512:(k + 1) * 512],
                                    start=(kk == 0), stop=(kk == 15))
                p.inc(ins, SP)
            wpm[wp] = p.mark(SP)

        @b.sync
        def _(sync, c=c):
            p.wait(sync, SP)
            p.dma(sync, QL, out=xtm[:, :],
                  in_=io["x_tm"][c * Q:(c + 1) * Q, :])
        yield

        @b.vector
        def _(v):
            p.wait(v, SP, QG, QG1)
            p.wait(v, QL)
            p.inc(v.tensor_tensor(xres, pO[:, 0:DM], xtm[:, :], OP.add),
                  SD)
            p.wait(v, SD)
            p.inc(v.tensor_scalar(dmp, xres, 0.0, 0.0, OP.add,
                                  OP.add, accum_out=sm[:, 0:1]), SD)
            p.wait(v, SD)
            p.inc(v.scalar_tensor_tensor(dmp, xres, 1.0,
                                         xres, OP.mult, OP.mult,
                                         accum_out=sm[:, 1:2]), SD)
            p.wait(v, SD)
            v.tensor_scalar(sm[:, 2:3], sm[:, 0:1], 1.0 / DM, None, OP.mult)
            p.inc(v.tensor_scalar(sm[:, 3:4], sm[:, 1:2], 1.0 / DM, None,
                                  OP.mult), SD)
            p.wait(v, SD)
            p.inc(v.tensor_tensor(sm[:, 4:5], sm[:, 2:3], sm[:, 2:3],
                                  OP.mult), SD)
            p.wait(v, SD)
            p.inc(v.tensor_tensor(sm[:, 5:6], sm[:, 3:4], sm[:, 4:5],
                                  OP.subtract), SD)

        @b.scalar
        def _(sc):
            p.wait(sc, SD)
            p.inc(sc.activation(sm[:, 6:7], sm[:, 5:6], AF.Ln,
                                bias=C["epsb"][:, 0:1]), SA)
            sc.wait_ge(p.sems[SA], p.ct[SA])
            p.inc(sc.activation(sm[:, 7:8], sm[:, 6:7], AF.Exp, scale=-0.5),
                  SA)

        @b.vector
        def _(v):
            p.wait(v, SA)
            p.inc(v.tensor_scalar(xres, xres, sm[:, 2:3],
                                  sm[:, 7:8], OP.subtract, OP.mult), SD)

        qx = (QG, QG1)[len(xlog) % 2]

        @b.gpsimd
        def _(g, c=c, qx=qx):
            p.wait(g, SD)
            p.dma(g, qx, out=io["sXLN"][c * Q:(c + 1) * Q, :], in_=xres)
        xlog.append((qx, p.mark(qx)))
        yield "c"


def gen_s6(p, io, C, T, xlog):
    b = p.block
    TT = 256
    u1 = T["ysb"]
    v1P = [T["mallP"][0], T["mallP"][1]]
    u2Q = [T["crow"][:, 0:2048], T["crow"][:, 2048:4096],
           T["xsckP"][0], T["xsckP"][1]]
    v2 = T["szck"]
    b2tP = [T["ddtm"], T["eactm"]]
    xlt = T["xsckP"][2]
    xlTP = [T["ttm"], T["stmp"]]
    pfm = T["actm"]
    qfm = T["asall"]
    hfmQ = [T["xdP"][0], T["xdP"][1], T["xddP"][0], T["xddP"][1]]
    o2 = T["ttm"]
    dmp = T["xdP"][0][:, 0:1024]
    sm = T["bckP"][0][:, 0:8]
    pAP = [T["pX"], T["pH1"]]

    def hf(ffc):
        return hfmQ[ffc // 8][:, (ffc % 8) * TT:(ffc % 8 + 1) * TT]

    def paT(c0, w):
        h = c0 // 1024
        return pAP[h][:, c0 - h * 1024:c0 - h * 1024 + w]

    @b.sync
    def _(sync):
        for nm in list(p.sems):
            if not nm.startswith("qsb"):
                p.wait(sync, nm)
        p.dma(sync, "qS",
            out=u1[:, :].rearrange("p (kk rc c) -> p kk rc c", kk=NKK, rc=2),
            in_=io["U1b"].rearrange("kk rc p c -> p kk rc c"))
        for rc in range(2):
            p.dma(sync, "qS",
                out=v1P[rc][:, :].rearrange("p (ff c) -> p ff c", ff=32),
                in_=io["V1b"][rc].rearrange("ff p c -> p ff c"))
        for q in range(4):
            p.dma(sync, "qS",
                out=u2Q[q][:, :].rearrange("p (ff rc c) -> p ff rc c",
                                           ff=8, rc=2),
                in_=io["U2b"][q * 8:(q + 1) * 8].rearrange(
                    "ff rc p c -> p ff rc c"))
        p.dma(sync, "qS",
            out=v2[:, :].rearrange("p (rc dm c) -> p rc dm c", rc=2, dm=8),
            in_=io["V2b"].rearrange("rc dm p c -> p rc dm c"))
    yield

    @b.sync
    def _(sync):
        p.wait(sync, "qS")
        for h in range(2):
            p.dma(sync, "qS",
                out=b2tP[h][:, :],
                in_=io["b2"][h * 512:(h + 1) * 512].unsqueeze(0)
                .partition_broadcast(128).squeeze(1))
    yield

    for tt in range(7, -1, -1):
        @b.sync
        def _(sync, tt=tt):
            p.wait(sync, "sPE", "sDV", "sAC", "qG")
            for idx in (15 - 2 * tt - 1, 15 - 2 * tt):
                sem, mk = xlog[idx]
                p.wait_at(sync, sem, mk)
            p.dma(sync, "qS",
                out=xlt[:, :].rearrange("p (tc d) -> p tc d", tc=2),
                in_=io["sXLN"][tt * TT:(tt + 1) * TT, :].rearrange(
                    "(tc p) d -> p tc d", p=128))
        yield

        @b.tensor
        def _(t_):
            p.wait(t_, "qS", "sDV")
            ins = None
            for kk in range(NKK):
                for tc in range(2):
                    j = kk * 2 + tc
                    ins = t_.transpose(
                        pAP[j // 8][:, (j % 8) * 128:(j % 8 + 1) * 128],
                        xlt[:, tc * DM + kk * 128:tc * DM + (kk + 1) * 128],
                        C["ident"][:, :])
            p.inc(ins, "sPE")

        @b.vector
        def _(v):
            p.wait(v, "sPE", "qG")
            for h in range(2):
                ins = v.tensor_copy(xlTP[h][:, :], pAP[h][:, :])
            p.inc(ins, "sDV")
        yield

        for rc in range(2):
            @b.tensor
            def _(t_, rc=rc):
                p.wait(t_, "sDV", "sAC")
                ins = None
                for kk in range(NKK):
                    ins = t_.matmul(paT(0, TT),
                                    u1[:, (kk * 2 + rc) * 128:
                                       (kk * 2 + rc + 1) * 128],
                                    xlTP[kk // 4][:, (kk % 4) * TT:
                                                  (kk % 4 + 1) * TT],
                                    start=(kk == 0), stop=(kk == NKK - 1))
                p.inc(ins, "sPE")

            @b.vector
            def _(v, rc=rc):
                p.wait(v, "sPE")
                p.inc(v.tensor_copy(pfm[:, rc * TT:(rc + 1) * TT],
                                    paT(0, TT)), "sDV")
        yield

        for fg in range(8):
            @b.tensor
            def _(t_, fg=fg):
                p.wait(t_, "sDV")
                if fg < 2:
                    p.wait(t_, "sAC")
                ins = None
                for j in range(4):
                    ffc = fg * 4 + j
                    for rc in range(2):
                        ins = t_.matmul(paT(512 + j * TT, TT),
                                        v1P[rc][:, ffc * 128:(ffc + 1) * 128],
                                        pfm[:, rc * TT:(rc + 1) * TT],
                                        start=(rc == 0), stop=(rc == 1))
                p.inc(ins, "sPE")
            fgm = p.mark("sPE")

            @b.vector
            def _(v, fg=fg, fgm=fgm):
                p.wait_at(v, "sPE", fgm)
                ins = None
                for j in range(4):
                    ffc = fg * 4 + j
                    ins = v.tensor_scalar(hf(ffc), paT(512 + j * TT, TT),
                                          C["b1r"][:, ffc:ffc + 1], None,
                                          OP.add)
                p.inc(ins, "sDV")
        yield

        @b.scalar
        def _(sc):
            p.wait(sc, "sDV")
            ins = None
            for q in range(4):
                ins = sc.activation(hfmQ[q][:, :], hfmQ[q][:, :],
                                    AF.Gelu_apprx_tanh)
            p.inc(ins, "sAC")

        for rc in range(2):
            @b.tensor
            def _(t_, rc=rc):
                p.wait(t_, "sAC", "sDV")
                ins = None
                for ffc in range(32):
                    ins = t_.matmul(paT(1024, TT),
                                    u2Q[ffc // 8][:, ((ffc % 8) * 2 + rc)
                                                  * 128:((ffc % 8) * 2 + rc
                                                         + 1) * 128],
                                    hf(ffc),
                                    start=(ffc == 0), stop=(ffc == 31))
                p.inc(ins, "sPE")

            @b.vector
            def _(v, rc=rc):
                p.wait(v, "sPE")
                p.inc(v.tensor_copy(qfm[:, rc * TT:(rc + 1) * TT],
                                    paT(1024, TT)), "sDV")
        yield

        for tc in range(2):
            @b.tensor
            def _(t_, tc=tc):
                p.wait(t_, "sDV", "sGP")
                ins = None
                for dmt in range(2):
                    for rc in range(2):
                        ins = t_.matmul(
                            paT(1024 + dmt * 512, 512),
                            qfm[:, rc * TT + tc * 128:rc * TT + (tc + 1) * 128],
                            v2[:, (rc * 8 + dmt * 4) * 128:
                               (rc * 8 + dmt * 4 + 4) * 128],
                            start=(rc == 0), stop=(rc == 1))
                p.inc(ins, "sPE")

            @b.vector
            def _(v, tc=tc):
                p.wait(v, "sPE", "qG")
                for h in range(2):
                    ins = v.tensor_tensor(o2[:, h * 512:(h + 1) * 512],
                                          pAP[1][:, h * 512:(h + 1) * 512],
                                          b2tP[h][:, :], OP.add)
                p.inc(ins, "sDV")
                p.wait(v, "sDV")
                p.inc(v.tensor_tensor(o2[:, :], o2[:, :],
                                      xlt[:, tc * DM:(tc + 1) * DM], OP.add),
                      "sDV")
                p.wait(v, "sDV")
                p.inc(v.tensor_scalar(dmp[:, :], o2[:, :], 0.0, 0.0, OP.add,
                                      OP.add, accum_out=sm[:, 0:1]), "sDV")
                p.wait(v, "sDV")
                p.inc(v.tensor_tensor(dmp[:, :], o2[:, :], o2[:, :], OP.mult),
                      "sDV")
                p.wait(v, "sDV")
                p.inc(v.tensor_scalar(dmp[:, :], dmp[:, :], 0.0, 0.0, OP.add,
                                      OP.add, accum_out=sm[:, 1:2]), "sDV")
                p.wait(v, "sDV")
                v.tensor_scalar(sm[:, 2:3], sm[:, 0:1], 1.0 / DM, None,
                                OP.mult)
                p.inc(v.tensor_scalar(sm[:, 3:4], sm[:, 1:2], 1.0 / DM, None,
                                      OP.mult), "sDV")
                p.wait(v, "sDV")
                p.inc(v.tensor_tensor(sm[:, 4:5], sm[:, 2:3], sm[:, 2:3],
                                      OP.mult), "sDV")
                p.wait(v, "sDV")
                p.inc(v.tensor_tensor(sm[:, 5:6], sm[:, 3:4], sm[:, 4:5],
                                      OP.subtract), "sDV")

            @b.scalar
            def _(sc):
                p.wait(sc, "sDV")
                p.inc(sc.activation(sm[:, 6:7], sm[:, 5:6], AF.Ln,
                                    bias=C["epsb"][:, 0:1]), "sAC")
                sc.wait_ge(p.sems["sAC"], p.ct["sAC"])
                p.inc(sc.activation(sm[:, 7:8], sm[:, 6:7], AF.Exp,
                                    scale=-0.5), "sAC")

            @b.vector
            def _(v):
                p.wait(v, "sAC")
                p.inc(v.tensor_scalar(o2[:, :], o2[:, :], sm[:, 2:3],
                                      sm[:, 7:8], OP.subtract, OP.mult),
                      "sDV")

            @b.gpsimd
            def _(g, tt=tt, tc=tc):
                p.wait(g, "sDV")
                p.dma(g, "qG",
                    out=io["out"][tt * TT + tc * 128:tt * TT + (tc + 1) * 128,
                                  :],
                    in_=o2[:, :])
            yield


def _blk(a, kk, cc):
    a = np.ascontiguousarray(a, dtype=np.float32)
    return np.ascontiguousarray(
        a.reshape(kk, 128, cc, 128).transpose(0, 2, 1, 3))


def prepare_inmaps(inputs):
    x = np.asarray(inputs["L0_attn_x"], dtype=np.float32)
    Win = np.asarray(inputs["L0_attn_W_in"], dtype=np.float32)
    Wout = np.asarray(inputs["L0_attn_W_out"], dtype=np.float32)
    WinTp = np.zeros((DM, DIPP), dtype=np.float32)
    WinTp[:, :4384] = Win.T
    WinT3 = np.ascontiguousarray(
        _blk(WinTp, NKK, NCC).transpose(1, 2, 0, 3)).reshape(NCC, 128, NKK * 128)
    WoutT2 = np.ascontiguousarray(
        Wout.T.reshape(16, 128, 2, 4, 128).transpose(2, 1, 0, 3, 4)
    ).reshape(2, 128, 8192)
    U1b = _blk(np.asarray(inputs["L0_ffn_U1"]), NKK, 2)
    V1b = _blk(np.asarray(inputs["L0_ffn_V1"]), 2, 32)
    U2b = _blk(np.asarray(inputs["L0_ffn_U2"]), 32, 2)
    V2b = _blk(np.asarray(inputs["L0_ffn_V2"]), 2, 8)
    wf = {k: np.ascontiguousarray(np.asarray(inputs["L0_attn_%s_f" % k]),
                                  dtype=np.float32)
          for k in ("conv_w", "conv_b", "dt_bias", "A_log", "D", "norm_w")}
    wr = {k: np.ascontiguousarray(np.asarray(inputs["L0_attn_%s_r" % k]),
                                  dtype=np.float32)
          for k in ("conv_w", "conv_b", "dt_bias", "A_log", "D", "norm_w")}
    b1 = np.ascontiguousarray(np.asarray(inputs["L0_ffn_b1"], dtype=np.float32))
    b2 = np.ascontiguousarray(np.asarray(inputs["L0_ffn_b2"], dtype=np.float32))

    in_maps = []
    for core in range(8):
        seq, half = core // 2, core % 2
        x_can = x[seq] if half == 0 else x[seq, ::-1]
        xT = np.ascontiguousarray(x_can.T)
        wA, wB = (wf, wr) if half == 0 else (wr, wf)
        m = {
            "xTf": xT,
            "x_tm": np.ascontiguousarray(x_can[:TOWN]),
            "WinT3": WinT3, "WoutT2": WoutT2,
            "U1b": U1b, "V1b": V1b, "U2b": U2b, "V2b": V2b,
            "b1": b1, "b2": b2,
            "fsel": np.full((128, 1), 1.0 - half, dtype=np.float32),
        }
        for tag, w in (("A", wA), ("B", wB)):
            m["cw" + tag] = w["conv_w"]
            m["cb" + tag] = w["conv_b"]
            m["dtb" + tag] = w["dt_bias"]
            m["Alog" + tag] = w["A_log"]
            m["D" + tag] = w["D"]
            m["nw" + tag] = w["norm_w"]
        in_maps.append(m)
    return in_maps


def kernel(**inputs):
    in_maps = prepare_inmaps(inputs)
    nc, _ = build_program()
    res = run_bass_kernel_spmd(nc, in_maps, core_ids=list(range(8)))
    out = np.zeros((4, 4096, 1024), dtype=np.float32)
    for core in range(8):
        seq, half = core // 2, core % 2
        o = np.asarray(res.results[core]["out"])
        if half == 0:
            out[seq, :TOWN] = o
        else:
            out[seq, TOWN:] = o[::-1]
    return out
